# version 15 (final)
# baseline (speedup 1.0000x reference)
import contextlib
import numpy as np
import ml_dtypes
import concourse.bass as bass
import concourse.mybir as mybir
from concourse.bass_utils import run_bass_kernel_spmd

F32 = mybir.dt.float32
BF16 = mybir.dt.bfloat16
AF = mybir.ActivationFunctionType
ALU = mybir.AluOpType
AX = mybir.AxisListType

D = 2048
S = 2048
NH = 32
NKV = 8
FF = 5632
NDC = D // 128
NFH = FF // 128
NJB = S // 512
EPS_LN = 1e-5
EPS_Q = 1e-8

EVENS = [0, 1, 2, 3, 8, 9, 10, 11, 16, 17, 18, 19, 24, 25, 26, 27]
ODDS = [4, 5, 6, 7, 12, 13, 14, 15, 20, 21, 22, 23, 28, 29, 30, 31]

_ENG_SEM = {"tensor": "sT", "vector": "sV", "scalar": "sS",
            "sync": "dS", "gpsimd": "dG"}


class Op:
    __slots__ = ("eng", "emit", "deps", "dma", "sem", "val", "inc_k", "waits",
                 "fsz")

    def __init__(self, eng, emit, deps, dma, sem=None, fsz=0):
        self.eng = eng
        self.emit = emit
        self.deps = [d for d in deps if d is not None]
        self.dma = dma
        self.sem = sem or _ENG_SEM[eng]
        self.val = 0
        self.inc_k = 0
        self.waits = []
        self.fsz = fsz


class PB:

    def __init__(self):
        self.ops = []

    def add(self, eng, emit, deps=(), dma=False, sem=None, out=None):
        fsz = 0
        if out is not None:
            try:
                fsz = int(out.free_size())
            except TypeError:
                fsz = int(out.free_size)
        op = Op(eng, emit, deps, dma, sem=sem, fsz=fsz)
        self.ops.append(op)
        return op

    def dma(self, eng, out, in_, deps=(), slot="d", accum=None):
        if accum is None:
            emit = lambda e, o=out, i=in_: e.dma_start(out=o, in_=i)
        else:
            emit = lambda e, o=out, i=in_, a=accum: e.dma_start(out=o, in_=i,
                                                               accum_op=a)
        return self.add(eng, emit, deps, dma=True, sem=f"D_{eng[:2]}_{slot}")

    def mm(self, out, lhsT, rhs, start, stop, deps=(), tp=None):
        return self.add("tensor",
                        lambda e, o=out, l=lhsT, r=rhs, st=start, sp=stop, tp=tp:
                        e.matmul(o, l, r, start=st, stop=sp, tile_position=tp), deps)

    def act(self, out, in_, func, deps=(), bias=..., scale=...):
        def f(e, o=out, i=in_, fn=func, b=bias, s=scale):
            kw = {}
            if b is not ...:
                kw["bias"] = b
            if s is not ...:
                kw["scale"] = s
            return e.activation(o, i, fn, **kw)
        return self.add("scalar", f, deps, out=out)

    def scopy(self, out, in_, deps=()):
        return self.add("scalar", lambda e, o=out, i=in_: e.copy(o, i), deps,
                        out=out)

    def tt(self, out, a, b, opk, deps=()):
        return self.add("vector",
                        lambda e, o=out, x=a, y=b, k=opk: e.tensor_tensor(o, x, y, k),
                        deps, out=out)

    def ts(self, out, in0, s1, s2, op0, op1, deps=()):
        return self.add("vector",
                        lambda e, o=out, i=in0, a=s1, b=s2, o0=op0, o1=op1:
                        e.tensor_scalar(o, i, a, b, o0, o1), deps, out=out)

    def stt(self, out, in0, sc, in1, op0, op1, deps=()):
        return self.add("vector",
                        lambda e, o=out, i=in0, s=sc, j=in1, o0=op0, o1=op1:
                        e.scalar_tensor_tensor(o, i, s, j, o0, o1), deps, out=out)

    def vcopy(self, out, in_, deps=()):
        return self.add("vector", lambda e, o=out, i=in_: e.tensor_copy(o, i),
                        deps, out=out)

    def red(self, out, in_, opk, deps=(), absval=None):
        return self.add("vector",
                        lambda e, o=out, i=in_, k=opk, av=absval:
                        e.tensor_reduce(o, i, AX.X, k, apply_absolute_value=av),
                        deps, out=out)

    def memset(self, ap, val, deps=()):
        return self.add("vector", lambda e, a=ap, v=val: e.memset(a, v), deps)

    def barrier(self):
        last = {}
        for op in self.ops:
            if op.emit is not None:
                last[op.sem] = op
        for eng in _ENG_SEM:
            deps = [o for sm, o in last.items() if o.eng != eng or o.dma]
            self.add(eng, None, deps)

    def dma_sems(self):
        return sorted({op.sem for op in self.ops if op.dma})

    def finish(self, block, sems):
        for op in self.ops:
            for d in op.deps:
                if d.inc_k == 0 and not d.dma:
                    d.inc_k = 1
        cnt = {s: 0 for s in set(_ENG_SEM.values())}
        for op in self.ops:
            cnt.setdefault(op.sem, 0)
        for op in self.ops:
            if op.dma:
                op.inc_k = 16
            if op.inc_k:
                cnt[op.sem] += op.inc_k
                op.val = cnt[op.sem]
        waited = {eng: {} for eng in _ENG_SEM}
        for op in self.ops:
            w = waited[op.eng]
            for d in op.deps:
                if d.eng == op.eng and not d.dma and (
                        d.eng == "tensor" or
                        (d.fsz >= 1024 and op.fsz >= 1024)):
                    continue
                assert d.inc_k > 0 and d.val > 0, "dep without inc"
                if w.get(d.sem, 0) < d.val:
                    op.waits.append((d.sem, d.val))
                    w[d.sem] = d.val
        for eng_name in _ENG_SEM:
            grp = [op for op in self.ops if op.eng == eng_name]
            if not grp:
                continue

            def body(eng, grp=grp):
                for op in grp:
                    for s, v in op.waits:
                        eng.wait_ge(sems[s], v)
                    if op.emit is None:
                        continue
                    ins = op.emit(eng)
                    if op.inc_k:
                        ins.then_inc(sems[op.sem], op.inc_k)
            getattr(block, eng_name)(body)


def build(debug=False):
    nc = bass.Bass("TRN2", target_bir_lowering=False)
    dt = nc.dram_tensor

    xt = dt("xt", [D, S], F32, kind="ExternalInput").ap()
    wq4 = dt("wq4", [NDC, 128, NDC, 128], F32, kind="ExternalInput").ap()
    wk4 = dt("wk4", [4, 128, NDC, 128], F32, kind="ExternalInput").ap()
    wv2 = dt("wv2", [D, 512], F32, kind="ExternalInput").ap()
    wo4 = dt("wo4", [NDC, 128, NDC, 128], F32, kind="ExternalInput").ap()
    w14 = dt("w14", [NFH, 128, NDC, 128], F32, kind="ExternalInput").ap()
    w24 = dt("w24", [NFH, 128, NDC, 128], F32, kind="ExternalInput").ap()
    w34 = dt("w34", [NDC, 128, NFH, 128], F32, kind="ExternalInput").ap()
    tabA = dt("tabA", [128, S], F32, kind="ExternalInput").ap()
    tabB = dt("tabB", [128, S], F32, kind="ExternalInput").ap()
    dmask = dt("dmask", [128, 128], BF16, kind="ExternalInput").ap()

    outT = dt("outT", [D, S], F32, kind="ExternalOutput").ap()
    x2s = dt("x2s", [D, S], F32, kind="Internal").ap()
    qs = dt("qs", [NJB, NDC, 128, 512], BF16, kind="Internal").ap()
    ys = dt("ys", [NDC, 128, S], BF16, kind="Internal").ap()
    qw1s = dt("qw1s", [NFH, 128, NDC, 128], BF16, kind="Internal").ap()
    qw2s = dt("qw2s", [NFH, 128, NDC, 128], BF16, kind="Internal").ap()
    qw3s = dt("qw3s", [NDC, 128, NFH, 128], BF16, kind="Internal").ap()
    gsums = dt("gsums", [3, 128, 1], F32, kind="Internal").ap()
    thrs = dt("thrs", [3, 1, 1], F32, kind="Internal").ap()

    dbg = {}
    if debug:
        for nm, shape, ddt in [
            ("d_xn", [D, S], BF16), ("d_kb", [512, S], BF16),
            ("d_vb", [128, NKV * 16 * 65], BF16), ("d_thr", [128, 3], F32),
            ("d_bias", [128, 16], F32), ("d_attn2", [D, S], BF16),
            ("d_kn", [1, 256], F32), ("d_xn2", [D, S], BF16),
            ("d_yh", [D, S], BF16),
            ("d_hid0", [128, NFH * 1024], BF16), ("d_ffn", [D, S], BF16),
        ]:
            dbg[nm] = dt(nm, shape, ddt, kind="ExternalOutput").ap()

    p = PB()
    es = contextlib.ExitStack()
    with es:
        E = es.enter_context
        block = E(nc.Block())
        sems = {}

        def SB(st, n, sh, d):
            return st.enter_context(nc.sbuf_tensor(n, sh, d))

        def PS(st, n, sh):
            return st.enter_context(nc.psum_tensor(n, sh, F32))

        ones_bf = SB(es, "ones_bf", [128, 1], BF16)
        ones_fr = SB(es, "ones_fr", [1, 128], F32)
        ones_fc = SB(es, "ones_fc", [128, 1], F32)
        onesr64 = SB(es, "onesr64", [65, 128], BF16)
        dmk = SB(es, "dmk", [128, 128], BF16)
        krows = SB(es, "krows", [1, 256], F32)
        srow = SB(es, "srow", [1, 96], F32)
        bias_all = SB(es, "bias_all", [128, 16], F32)
        thrP = SB(es, "thrP", [128, 3], F32)
        thrN = SB(es, "thrN", [128, 3], F32)
        lnr = SB(es, "lnr", [65, 512], F32)
        epsln = SB(es, "epsln", [1, 1], F32)
        recb = SB(es, "recb", [65, 512], BF16)
        KT, KN2, KREP, QM4, QM2 = 0, 32, 64, 96, 224

        ld_mask = p.dma("sync", dmk[:], dmask, slot="msk")
        p.memset(ones_bf[:], 1.0)
        p.memset(ones_fr[:], 1.0)
        p.memset(ones_fc[:], 1.0)
        p.memset(onesr64[:], 0.0)
        c_ones64 = p.memset(onesr64[64:65, :], 1.0)
        c_eps = p.memset(epsln[:], EPS_LN)

        W_SLICES = []
        for wap, qap in [(w14, qw1s), (w24, qw2s)]:
            sl = []
            for i in range(NFH):
                for h in range(2):
                    sl.append((wap[i, :, h * 8:(h + 1) * 8, :]
                               .rearrange("p a c -> p (a c)"),
                               qap[i, :, h * 8:(h + 1) * 8, :]
                               .rearrange("p a c -> p (a c)"), 1024))
            W_SLICES.append(sl)
        sl3 = []
        for i in range(NDC):
            for h in range(4):
                sl3.append(
                    (w34[i, :, h * 11:(h + 1) * 11, :].rearrange("p a c -> p (a c)"),
                     qw3s[i, :, h * 11:(h + 1) * 11, :].rearrange("p a c -> p (a c)"),
                     11 * 128))
        W_SLICES.append(sl3)

        qscope = contextlib.ExitStack()

        def SBR(st, n, sh, d):
            return st.enter_context(nc.sbuf_tensor(n, sh, d, side="right"))
        gbuf = [SBR(qscope, f"gbuf{i}", [128, 1408], F32) for i in range(3)]
        gcol = SBR(qscope, "gcol", [128, 96], F32)
        g128 = SBR(qscope, "g128", [128, 1], F32)
        grow = SBR(qscope, "grow", [1, 1], F32)
        growr = SBR(qscope, "growr", [1, 128], F32)
        qtmp = SBR(qscope, "qtmp", [128, 1408], F32)
        qout = [SBR(qscope, f"qout{i}", [128, 1408], BF16) for i in range(2)]

        quant_done = [None, None]
        _qd_state = {"done": False}

        def wstream_gen():
            thr_ev = []
            for wi, slices in enumerate(W_SLICES):
                reds = []
                lds = []
                for i in range(len(slices) + 2):
                    if i < len(slices):
                        srcw, _q, wid = slices[i]
                        lds.append(p.dma("gpsimd", gbuf[i % 3][:, 0:wid], srcw,
                                         deps=[reds[i - 3]] if i >= 3 else [],
                                         slot=f"g{i % 3}"))
                    j = i - 2
                    if j >= 0:
                        wid = slices[j][2]
                        reds.append(p.red(gcol[:, j:j + 1],
                                          gbuf[j % 3][:, 0:wid], ALU.add,
                                          deps=[lds[j]], absval=True))
                    yield
                rsum = p.red(g128[:], gcol[:, 0:len(slices)], ALU.add, deps=reds)
                st1 = p.dma("gpsimd", gsums[wi], g128[:], deps=[rsum],
                            slot="thr")
                ldr = p.dma("gpsimd", growr[:],
                            gsums[wi].rearrange("p x -> x (p)"), deps=[st1],
                            slot="thr")
                r2 = p.red(grow[:], growr[:], ALU.add, deps=[ldr])
                t1 = p.ts(grow[:], grow[:], 0.5 / (FF * D), 0.5 * EPS_Q,
                          ALU.mult, ALU.add, deps=[r2])
                st2 = p.dma("gpsimd", thrs[wi], grow[:], deps=[t1],
                            slot="thr")
                bcl = p.dma("gpsimd", thrP[:, wi:wi + 1],
                            thrs[wi].broadcast_to([128, 1]), deps=[st2],
                            slot="thr")
                tn = p.ts(thrN[:, wi:wi + 1], thrP[:, wi:wi + 1], -1.0, None,
                          ALU.mult, ALU.bypass, deps=[bcl])
                thr_ev.append((bcl, tn))
                yield
            prev_store = [None, None]
            prev_pass2 = [None, None, None]
            ALLQ = [(wi, srcw, qdst, wid) for wi, slices in enumerate(W_SLICES)
                    for (srcw, qdst, wid) in slices]
            lds = []
            last_st = None
            for i in range(len(ALLQ) + 2):
                if i < len(ALLQ):
                    wi, srcw, qdst, wid = ALLQ[i]
                    lds.append(p.dma("gpsimd", gbuf[i % 3][:, 0:wid], srcw,
                                     deps=[prev_pass2[i % 3]],
                                     slot=f"g{i % 3}"))
                j = i - 2
                if j >= 0:
                    wi, srcw, qdst, wid = ALLQ[j]
                    p1 = p.ts(qtmp[:, 0:wid], gbuf[j % 3][:, 0:wid],
                              thrP[:, wi:wi + 1], 1.0, ALU.is_gt, ALU.subtract,
                              deps=[lds[j], thr_ev[wi][0], thr_ev[wi][1],
                                    prev_store[j % 2]])
                    p2 = p.stt(qout[j % 2][:, 0:wid], gbuf[j % 3][:, 0:wid],
                               thrN[:, wi:wi + 1], qtmp[:, 0:wid],
                               ALU.is_gt, ALU.add, deps=[p1])
                    st = p.dma("gpsimd", qdst, qout[j % 2][:, 0:wid],
                               deps=[p2], slot=f"q{j % 2}")
                    prev_store[j % 2] = st
                    prev_pass2[j % 3] = p2
                    last_st = st
                yield
            quant_done[0] = last_st
            quant_done[1] = prev_store[(len(ALLQ) - 2) % 2]
            _qd_state["done"] = True

        _wgen = wstream_gen()

        def drive(n=1):
            for _ in range(n):
                if next(_wgen, "END") == "END":
                    break

        def drain_wstream():
            while not _qd_state["done"]:
                if next(_wgen, "END") == "END":
                    break


        def ln_rows(rows, pz, pq2, width, deps):
            RM, RX2, RMS, RR = 0, width, 2 * width, 3 * width
            RV = RX2
            RNR = 4 * width
            vops = []
            for ap_, c in pz:
                vops.append(p.ts(rows[0:1, RM + c:RM + c + 512], ap_, 1.0 / D, None,
                                 ALU.mult, ALU.bypass, deps=deps))
            for ap_, c in pq2:
                vops.append(p.ts(rows[0:1, RX2 + c:RX2 + c + 512], ap_, 1.0 / D,
                                 None, ALU.mult, ALU.bypass, deps=deps))
            m_ = rows[0:1, RM:RM + width]
            msq = p.tt(rows[0:1, RMS:RMS + width], m_, m_, ALU.mult, deps=vops)
            vv = p.tt(rows[0:1, RV:RV + width], rows[0:1, RX2:RX2 + width],
                      rows[0:1, RMS:RMS + width], ALU.subtract, deps=[msq])
            a1 = p.act(rows[0:1, RMS:RMS + width], rows[0:1, RV:RV + width],
                       AF.Ln, deps=[vv, c_eps], bias=epsln[:])
            a2 = p.act(rows[0:1, RR:RR + width], rows[0:1, RMS:RMS + width],
                       AF.Exp, deps=[a1], scale=-0.5)
            return p.stt(rows[0:1, RNR:RNR + width], rows[0:1, RM:RM + width],
                         -1.0, rows[0:1, RR:RR + width], ALU.mult, ALU.mult,
                         deps=[a2])

        def bcast_rows(rows, pbr, pbn, width, deps):
            RR, RNR = 3 * width, 4 * width
            outs = []
            for j in range(width // 512):
                c = j * 512
                outs.append(p.mm(pbr[:, c:c + 512], ones_fr[:],
                                 rows[0:1, RR + c:RR + c + 512], True, True,
                                 deps=deps))
                outs.append(p.mm(pbn[:, c:c + 512], ones_fr[:],
                                 rows[0:1, RNR + c:RNR + c + 512], True, True,
                                 deps=deps))
            return outs

        with contextlib.ExitStack() as att:
            kb = [SB(att, f"kb{m}", [128, S], BF16) for m in range(4)]
            vb = [SB(att, f"vb{g}", [128, 16 * 65], BF16) for g in range(NKV)]
            vb_init = [p.memset(vb[g][:], 1.0) for g in range(NKV)]

            with contextlib.ExitStack() as lB:
                xn = [SB(lB, f"xn{i}", [128, S], BF16) for i in range(NDC)]
                tA = SB(lB, "tA", [128, S], F32)
                tB = SB(lB, "tB", [128, S], F32)
                ld_tA = p.dma("sync", tA[:], tabA, slot="tA")
                ld_tB = p.dma("sync", tB[:], tabB, slot="tB")

                p12 = contextlib.ExitStack()
                rows1 = SB(p12, "rows1", [1, 5 * S], F32)
                with contextlib.ExitStack() as ps:
                    xsq = [SB(ps, f"xsq{i}", [128, S], BF16) for i in range(2)]
                    pz = [PS(ps, f"pz{j}", [1, 512]) for j in range(NJB)]
                    pq2 = [PS(ps, f"pq2{j}", [1, 512]) for j in range(NJB)]
                    x_ld = [p.dma("gpsimd", xn[dc][:],
                                  xt[dc * 128:(dc + 1) * 128, :], slot="xn")
                            for dc in range(NDC)]
                    stat = []
                    for dc in range(NDC):
                        sq = p.act(xsq[dc % 2][:], xn[dc][:], AF.Square,
                                   deps=[x_ld[-1]] +
                                        ([stat[dc - 2]] if dc >= 2 else []))
                        last = None
                        for j in range(NJB):
                            p.mm(pz[j][:], ones_bf[:],
                                 xn[dc][:, j * 512:(j + 1) * 512],
                                 dc == 0, dc == NDC - 1,
                                 deps=[sq] + ([c_eps] if dc == 0 else []))
                            last = p.mm(pq2[j][:], ones_bf[:],
                                        xsq[dc % 2][:, j * 512:(j + 1) * 512],
                                        dc == 0, dc == NDC - 1, deps=[sq])
                        stat.append(last)
                    nr1 = ln_rows(rows1, [(pz[j][:], j * 512) for j in range(NJB)],
                                  [(pq2[j][:], j * 512) for j in range(NJB)],
                                  S, deps=stat)

                with contextlib.ExitStack() as ps:
                    pbr = PS(ps, "pbr", [128, S])
                    pbn = PS(ps, "pbn", [128, S])
                    tmpf = [SB(ps, f"tmpf{i}", [128, S], F32) for i in range(2)]
                    bc = bcast_rows(rows1, pbr, pbn, S, deps=[nr1])
                    xn_ev = []
                    for dc in range(NDC):
                        t1 = p.tt(tmpf[dc % 2][:], xn[dc][:], pbr[:], ALU.mult,
                                  deps=bc if dc == 0 else [])
                        xn_ev.append(p.tt(xn[dc][:], tmpf[dc % 2][:], pbn[:],
                                          ALU.add, deps=[t1]))
                p12.close()
                if debug:
                    for dc in range(NDC):
                        p.dma("sync", dbg["d_xn"][dc * 128:(dc + 1) * 128, :],
                              xn[dc][:], deps=[xn_ev[dc]], slot="dbg")
                p.barrier()

                with contextlib.ExitStack() as kns:
                    pkn = [PS(kns, f"pkn{i}", [1, 512]) for i in range(2)]

                    with contextlib.ExitStack() as ps:
                        wkb = [SB(ps, f"wkb{i}", [128, S], BF16) for i in range(2)]
                        wvb = SB(ps, "wvb", [128, NDC * 512], BF16)
                        kraw = [SB(ps, f"kraw{i}", [128, 1024], F32)
                                for i in range(2)]
                        kswap = SB(ps, "kswap", [128, 1024], F32)
                        ksq = SB(ps, "ksq", [128, S], BF16)
                        pkv = [PS(ps, f"pkv{i}", [128, 1024]) for i in range(2)]
                        pv = [PS(ps, f"pv{i}", [128, 512]) for i in range(2)]

                        k_mm, k_evac, k_swap, k_rope = [], [], [], []
                        for m in range(4):
                            ld = p.dma("gpsimd",
                                       wkb[m % 2][:].rearrange("p (a c) -> p a c",
                                                               c=128),
                                       wk4[m],
                                       deps=[k_mm[2 * m - 3]] if m >= 2 else [],
                                       slot=f"wk{m % 2}")
                            for hf in range(2):
                                step = 2 * m + hf
                                last = None
                                for dc in range(NDC):
                                    for j in range(2):
                                        c = hf * 1024 + j * 512
                                        last = p.mm(
                                            pkv[step % 2][:, j * 512:(j + 1) * 512],
                                            wkb[m % 2][:, dc * 128:(dc + 1) * 128],
                                            xn[dc][:, c:c + 512],
                                            dc == 0, dc == NDC - 1,
                                            deps=[ld, xn_ev[dc]] +
                                                 ([k_evac[step - 2]]
                                                  if step >= 2 and dc == 0 else []))
                                k_mm.append(last)
                                ev = p.scopy(kraw[step % 2][:], pkv[step % 2][:],
                                             deps=[last] +
                                                  ([k_swap[step - 2]]
                                                   if step >= 2 else []) +
                                                  ([k_rope[step - 2]]
                                                   if step >= 2 else []))
                                k_evac.append(ev)
                                sw = None
                                for a in range(4):
                                    sw = p.dma(
                                        "gpsimd",
                                        kswap[(a ^ 1) * 32:(a ^ 1) * 32 + 32, :],
                                        kraw[step % 2][a * 32:a * 32 + 32, :],
                                        deps=[ev] + ([k_rope[step - 1]]
                                                     if step >= 1 else []),
                                        slot="swp")
                                k_swap.append(sw)
                                c = hf * 1024
                                tsw = p.tt(kswap[:], kswap[:], tB[:, c:c + 1024],
                                           ALU.mult, deps=[sw, ld_tB])
                                t2 = p.tt(kraw[step % 2][:], kraw[step % 2][:],
                                          tA[:, c:c + 1024], ALU.mult,
                                          deps=[ld_tA])
                                k_rope.append(
                                    p.tt(kb[m][:, c:c + 1024], kraw[step % 2][:],
                                         kswap[:], ALU.add, deps=[t2, tsw]))

                        kn_red = []
                        kn_mm_last = None
                        for m in range(4):
                            sq = p.act(ksq[:], kb[m][:], AF.Square,
                                       deps=[k_rope[2 * m], k_rope[2 * m + 1]] +
                                            ([kn_mm_last] if kn_mm_last else []))
                            for gh in range(2):
                                g = 2 * m + gh
                                p0 = 64 * gh
                                for j in range(NJB):
                                    slot = (g * NJB + j) % 2
                                    mm = p.mm(pkn[slot][:], ones_bf[p0:p0 + 64, :],
                                              ksq[p0:p0 + 64,
                                                  j * 512:(j + 1) * 512],
                                              True, True,
                                              deps=[sq] +
                                                   ([kn_red[-2]]
                                                    if len(kn_red) >= 2 else []))
                                    kn_mm_last = mm
                                    kn_red.append(p.red(
                                        krows[0:1, KT + 4 * g + j:
                                              KT + 4 * g + j + 1],
                                        pkn[slot][:], ALU.max, deps=[mm]))
                        kn2_ops = [p.red(krows[0:1, KN2 + g:KN2 + g + 1],
                                         krows[0:1, KT + 4 * g:KT + 4 * g + 4],
                                         ALU.max, deps=[kn_red[-1]])
                                   for g in range(NKV)]
                        src = krows[0:1, KN2:KN2 + 8].rearrange(
                            "r (x c) -> r x c", c=2)
                        src = src.unsqueeze(2).broadcast_to([1, 4, 4, 2])
                        knrep = p.vcopy(
                            krows[0:1, KREP:KREP + 32].rearrange(
                                "r (x b c) -> r x b c", b=4, c=2), src,
                            deps=kn2_ops)

                        ld_wv = p.dma("gpsimd",
                                      wvb[:].rearrange("p (a c) -> p a c", c=512),
                                      wv2.rearrange("(a p) c -> p a c", p=128),
                                      slot="wv")
                        v_ev = []
                        for tc in range(16):
                            last = None
                            for dc in range(NDC):
                                last = p.mm(pv[tc % 2][:],
                                            xn[dc][:, tc * 128:(tc + 1) * 128],
                                            wvb[:, dc * 512:(dc + 1) * 512],
                                            dc == 0, dc == NDC - 1,
                                            deps=[ld_wv, xn_ev[dc]] +
                                                 ([v_ev[tc - 2]]
                                                  if tc >= 2 and dc == 0 else []))
                            ev = None
                            for g in range(NKV):
                                ev = p.scopy(vb[g][:, tc * 65:tc * 65 + 64],
                                             pv[tc % 2][:, g * 64:(g + 1) * 64],
                                             deps=[last] +
                                                  (vb_init if tc == 0 else []))
                            v_ev.append(ev)

                        if debug:
                            for m in range(4):
                                p.dma("sync",
                                      dbg["d_kb"][m * 128:(m + 1) * 128, :],
                                      kb[m][:], deps=[k_rope[2 * m + 1]])
                            for g in range(NKV):
                                p.dma("sync",
                                      dbg["d_vb"][:, g * 1040:(g + 1) * 1040],
                                      vb[g][:], deps=[v_ev[-1]])
                    p.barrier()

                    with contextlib.ExitStack() as ps:
                        wqb = [SB(ps, f"wqb{i}", [128, S], BF16) for i in range(2)]
                        qraw = [SB(ps, f"qraw{i}", [128, S], F32) for i in range(2)]
                        qswap = SB(ps, "qswap", [128, S], F32)
                        qrot = [SB(ps, f"qrot{i}", [128, S], BF16)
                                for i in range(2)]
                        qsq = SB(ps, "qsq", [128, S], BF16)
                        pq = [PS(ps, f"pq{i}", [128, 1024]) for i in range(2)]

                        q_mmL, q_evac, q_swap, q_rope, q_store, qn_red, q_sq = \
                            [], [], [], [], [], [], []
                        for mc in range(NDC):
                            ld = p.dma("gpsimd",
                                       wqb[mc % 2][:].rearrange(
                                           "p (a c) -> p a c", c=128),
                                       wq4[mc],
                                       deps=[q_mmL[mc - 2]] if mc >= 2 else [],
                                       slot=f"wq{mc % 2}")
                            for hf in range(2):
                                step = 2 * mc + hf
                                last = None
                                for dc in range(NDC):
                                    for j in range(2):
                                        c = hf * 1024 + j * 512
                                        last = p.mm(
                                            pq[step % 2][:, j * 512:(j + 1) * 512],
                                            wqb[mc % 2][:, dc * 128:(dc + 1) * 128],
                                            xn[dc][:, c:c + 512],
                                            dc == 0, dc == NDC - 1,
                                            deps=[ld, xn_ev[dc]] +
                                                 ([q_evac[step - 2]]
                                                  if step >= 2 and dc == 0 else []))
                                ev = p.scopy(qraw[mc % 2][:, hf * 1024:
                                                          hf * 1024 + 1024],
                                             pq[step % 2][:],
                                             deps=[last] +
                                                  ([q_swap[mc - 2],
                                                    q_rope[mc - 2]]
                                                   if mc >= 2 and hf == 0 else []))
                                q_evac.append(ev)
                            q_mmL.append(last)
                            sw = None
                            for a in range(4):
                                sw = p.dma("gpsimd",
                                           qswap[(a ^ 1) * 32:(a ^ 1) * 32 + 32, :],
                                           qraw[mc % 2][a * 32:a * 32 + 32, :],
                                           deps=[q_evac[-1], q_evac[-2]] +
                                                ([q_rope[mc - 1]]
                                                 if mc >= 1 else []),
                                           slot="swp")
                            q_swap.append(sw)
                            tsw = p.tt(qswap[:], qswap[:], tB[:], ALU.mult,
                                       deps=[sw, ld_tB])
                            t2 = p.tt(qraw[mc % 2][:], qraw[mc % 2][:], tA[:],
                                      ALU.mult, deps=[ld_tA])
                            qr = p.tt(qrot[mc % 2][:], qraw[mc % 2][:], qswap[:],
                                      ALU.add,
                                      deps=[t2, tsw] +
                                           ([q_store[mc - 2], q_sq[mc - 2]]
                                            if mc >= 2 else []))
                            q_rope.append(qr)
                            st = None
                            for qb in range(NJB):
                                st = p.dma("sync", qs[qb, mc],
                                           qrot[mc % 2][:, qb * 512:(qb + 1) * 512],
                                           deps=[qr], slot="qst")
                            q_store.append(st)
                            sq = p.act(qsq[:], qrot[mc % 2][:], AF.Square,
                                       deps=[qr] +
                                            ([qn_red[-1]] if qn_red else []))
                            q_sq.append(sq)
                            for hh in range(2):
                                pos = 2 * mc + hh
                                for j in range(NJB):
                                    slot = (pos * NJB + j) % 2
                                    mm = p.mm(pkn[slot][:],
                                              ones_bf[hh * 64:hh * 64 + 64, :],
                                              qsq[hh * 64:hh * 64 + 64,
                                                  j * 512:(j + 1) * 512],
                                              True, True,
                                              deps=[sq, knrep] +
                                                   ([qn_red[-2]]
                                                    if len(qn_red) >= 2 else []))
                                    qn_red.append(p.red(
                                        krows[0:1, QM4 + 4 * pos + j:
                                              QM4 + 4 * pos + j + 1],
                                        pkn[slot][:], ALU.max, deps=[mm]))

            qm2_last = None
            for pos in range(NH):
                qm2_last = p.red(krows[0:1, QM2 + pos:QM2 + pos + 1],
                                 krows[0:1, QM4 + 4 * pos:QM4 + 4 * pos + 4],
                                 ALU.max, deps=[qn_red[-1]])
            s_a = p.ts(srow[0:1, 0:32], krows[0:1, QM2:QM2 + 32], 0.5, None,
                       ALU.mult, ALU.bypass, deps=[qm2_last])
            s_b = p.stt(srow[0:1, 32:64], krows[0:1, KREP:KREP + 32], 0.5,
                        srow[0:1, 0:32], ALU.mult, ALU.add,
                        deps=[s_a, knrep])
            s_c = p.tt(srow[0:1, 64:80], srow[0:1, 32:64:2],
                       srow[0:1, 33:64:2], ALU.max, deps=[s_b])
            sb16 = p.ts(srow[0:1, 80:96], srow[0:1, 64:80], -1.0, None,
                        ALU.mult, ALU.bypass, deps=[s_c])
            with contextlib.ExitStack() as ps:
                pbias = PS(ps, "pbias", [128, 16])
                mmb = p.mm(pbias[:], ones_fr[:], srow[0:1, 80:96], True, True,
                           deps=[sb16])
                bias_ev = p.vcopy(bias_all[:], pbias[:], deps=[mmb])
            if debug:
                p.dma("sync", dbg["d_bias"], bias_all[:], deps=[bias_ev], slot="dbg")
                p.dma("sync", dbg["d_kn"], krows[0:1, 0:256], deps=[bias_ev], slot="dbg")
            p.barrier()

            with contextlib.ExitStack() as ps:
                qsl = [SB(ps, f"qsl{t}", [128, 512], BF16) for t in range(NDC)]
                Phat = [SB(ps, f"Phat{i}", [128, 1024], BF16) for i in range(3)]
                oubT = [SB(ps, f"oubT{t}", [128, 512], BF16) for t in range(NDC)]
                ybuf = [SB(ps, f"ybuf{i}", [128, 512], BF16) for i in range(2)]
                srows = SB(ps, "srows", [65, 32 * 512], F32)
                recb = SB(ps, "recb2", [65, 1024], BF16)
                pS = [PS(ps, f"pS{i}", [128, 1024]) for i in range(2)]
                ouA = [PS(ps, f"ouA{i}", [65, 512]) for i in range(2)]
                ouB = [PS(ps, f"ouB{i}", [65, 512]) for i in range(2)]

                q_last_user = {t: None for t in range(NDC)}
                exp_bybuf = [None, None]
                pv_byphat = [None, None, None]
                ou_free = [[], []]
                scores_war = []
                ymt = [None] * NDC
                yst = [None] * NDC
                sum_cp = []
                for qb in range(NJB):
                    q_ld = []
                    for t in range(NDC):
                        q_ld.append(p.dma("sync", qsl[t][:], qs[qb, t],
                                          deps=[q_store[t], q_last_user[t]],
                                          slot="qsl"))
                    nkc = 4 * qb + 4
                    seq = [(t, kc) for t in range(NDC) for kc in range(nkc)]
                    pend = []
                    for i in range(len(seq) + 2):
                        if i < len(seq):
                            t, kc = seq[i]
                            hA, hB = EVENS[t], ODDS[t]
                            m = hA // 8
                            off = max(0, 128 * (kc - 4 * qb))
                            buf = i % 2
                            pbuf = i % 3
                            mA = p.mm(pS[buf][:, off:512],
                                      kb[m][0:64, kc * 128:(kc + 1) * 128],
                                      qsl[t][0:64, off:512], True, True,
                                      deps=[q_ld[-1], k_rope[2 * m],
                                            k_rope[2 * m + 1],
                                            exp_bybuf[buf]] +
                                           (scores_war if t == 0 and kc <= 1
                                            else []))
                            mB = p.mm(pS[buf][:, 512 + off:1024],
                                      kb[m][64:128, kc * 128:(kc + 1) * 128],
                                      qsl[t][64:128, off:512], True, True,
                                      deps=[], tp=(64, 0))
                            q_last_user[t] = mB
                            edeps = [mB, bias_ev, pv_byphat[pbuf]]
                            if off == 0:
                                eB = p.act(Phat[pbuf][:], pS[buf][:], AF.Exp,
                                           deps=edeps,
                                           bias=bias_all[:, t:t + 1])
                            else:
                                p.act(Phat[pbuf][:, off:512],
                                      pS[buf][:, off:512], AF.Exp, deps=edeps,
                                      bias=bias_all[:, t:t + 1])
                                eB = p.act(Phat[pbuf][:, 512 + off:1024],
                                           pS[buf][:, 512 + off:1024],
                                           AF.Exp, deps=[],
                                           bias=bias_all[:, t:t + 1])
                            exp_bybuf[buf] = eB
                            pv_dep = [eB]
                            if kc >= 4 * qb:
                                p.tt(Phat[pbuf][:, off:off + 128],
                                     Phat[pbuf][:, off:off + 128], dmk[:],
                                     ALU.mult, deps=[eB, ld_mask])
                                mkB = p.tt(
                                    Phat[pbuf][:, 512 + off:512 + off + 128],
                                    Phat[pbuf][:, 512 + off:512 + off + 128],
                                    dmk[:], ALU.mult, deps=[])
                                pv_dep = [mkB]
                            pend.append((pv_dep, pbuf, off, t, kc))
                            drive(1)
                        j = i - 2
                        if j >= 0:
                            pv_dep, pbuf, off, t, kc = pend[j]
                            os_ = t % 2
                            gA, gB = EVENS[t] // 4, ODDS[t] // 4
                            pvA = p.mm(ouA[os_][:, off:512],
                                       vb[gA][:, 65 * kc:65 * kc + 65],
                                       Phat[pbuf][:, off:512],
                                       kc == 0, kc == nkc - 1,
                                       deps=pv_dep + [v_ev[kc]] +
                                            (ou_free[os_] if kc == 0 else []))
                            pvB = p.mm(ouB[os_][:, off:512],
                                       vb[gB][:, 65 * kc:65 * kc + 65],
                                       Phat[pbuf][:, 512 + off:1024],
                                       kc == 0, kc == nkc - 1, deps=[])
                            pv_byphat[pbuf] = pvB
                            if kc == nkc - 1:
                                cpA = p.vcopy(oubT[t][0:64, :],
                                              ouA[os_][0:64, :],
                                              deps=[pvB, ymt[t]])
                                cpB = p.vcopy(oubT[t][64:128, :],
                                              ouB[os_][0:64, :], deps=[])
                                sA = p.vcopy(
                                    srows[64:65,
                                          (2 * t) * 512:(2 * t + 1) * 512],
                                    ouA[os_][64:65, :], deps=sum_cp[-1:])
                                sB = p.vcopy(
                                    srows[64:65,
                                          (2 * t + 1) * 512:(2 * t + 2) * 512],
                                    ouB[os_][64:65, :], deps=[])
                                sum_cp.append(sB)
                                ou_free[os_] = [cpA, cpB, sA, sB]
                    lnb = p.act(srows[64:65, :], srows[64:65, :], AF.Ln,
                                deps=[sum_cp[-1]])
                    exb = p.act(srows[64:65, :], srows[64:65, :], AF.Exp,
                                deps=[lnb], scale=-1.0)
                    ym_prev = []
                    scores_war = []
                    for t in range(NDC):
                        ct = p.vcopy(recb[64:65, :],
                                     srows[64:65, 2 * t * 512:(2 * t + 2) * 512],
                                     deps=[exb] + ym_prev)
                        bcA = p.mm(pS[0][:, 0:512], onesr64[64:65, :],
                                   recb[64:65, 0:512], True, True,
                                   deps=[ct, c_ones64, exp_bybuf[0]])
                        bcB = p.mm(pS[1][:, 0:512], onesr64[64:65, :],
                                   recb[64:65, 512:1024], True, True,
                                   deps=[ct, exp_bybuf[1]])
                        ymA = p.tt(ybuf[t % 2][0:64, :], oubT[t][0:64, :],
                                   pS[0][0:64, 0:512], ALU.mult,
                                   deps=[bcA, yst[t]])
                        ymB = p.tt(ybuf[t % 2][64:128, :], oubT[t][64:128, :],
                                   pS[1][0:64, 0:512], ALU.mult,
                                   deps=[bcB])
                        ym_prev = [ymA, ymB]
                        ymt[t] = ymB
                        scores_war += [ymA, ymB]
                        yst[t] = p.dma(
                            "sync", ys[t, :, qb * 512:(qb + 1) * 512],
                            ybuf[t % 2][:], deps=[ymA, ymB],
                            slot=f"ys{t % 2}")
                    scores_war = scores_war[-2:]
                y_done = [y for y in yst if y is not None][-2:]
            p.barrier()

        attn_ev = []
        for hf in range(2):
            c0 = hf * 1024
            with contextlib.ExitStack() as ps:
                yh = [SB(ps, f"yh{hf}_{dc}", [128, 1024], BF16) for dc in range(NDC)]
                wob = [SB(ps, f"wob{hf}_{i}", [128, S], BF16) for i in range(2)]
                abf = [SB(ps, f"abf{hf}_{dc}", [128, 1024], BF16) for dc in range(NDC)]
                sqb = [SB(ps, f"sqb{hf}_{i}", [128, 1024], BF16) for i in range(2)]
                xbuf = [SB(ps, f"xbuf{hf}_{i}", [128, 1024], F32) for i in range(2)]
                tbuf = [SB(ps, f"tbuf{hf}_{i}", [128, 1024], F32) for i in range(2)]
                x2b = [SB(ps, f"x2b{hf}_{i}", [128, 1024], F32) for i in range(2)]
                pwo = [PS(ps, f"pwo{hf}_{i}", [128, 1024]) for i in range(2)]
                pz6 = [PS(ps, f"pz6{hf}_{j}", [1, 512]) for j in range(2)]
                pq6 = [PS(ps, f"pq6{hf}_{j}", [1, 512]) for j in range(2)]
                rows6 = SB(ps, f"rows6{hf}", [1, 5 * 1024], F32)

                y_ld = [p.dma("sync", yh[dc][:], ys[dc, :, c0:c0 + 1024],
                              deps=y_done, slot="yh") for dc in range(NDC)]
                if debug:
                    for dc in range(NDC):
                        p.dma("sync", dbg["d_yh"][dc * 128:(dc + 1) * 128,
                                                  c0:c0 + 1024],
                              yh[dc][:], deps=[y_ld[dc]], slot="dbg")
                evs, sqs, stat6 = [], [], []
                for mc in range(NDC):
                    ld = p.dma("gpsimd",
                               wob[mc % 2][:].rearrange("p (a c) -> p a c", c=128),
                               wo4[mc],
                               deps=[evs[mc - 2]] if mc >= 2 else [],
                               slot=f"wo{mc % 2}")
                    last = None
                    for dc in range(NDC):
                        for j in range(2):
                            last = p.mm(pwo[mc % 2][:, j * 512:(j + 1) * 512],
                                        wob[mc % 2][:, dc * 128:(dc + 1) * 128],
                                        yh[dc][:, j * 512:(j + 1) * 512],
                                        dc == 0, dc == NDC - 1,
                                        deps=[ld, y_ld[-1]] +
                                             ([sqs[mc - 2]] if mc >= 2 and dc == 0
                                              else []))
                    ev = p.scopy(abf[mc][:], pwo[mc % 2][:], deps=[last])
                    sq = p.act(sqb[mc % 2][:], pwo[mc % 2][:], AF.Square,
                               deps=[stat6[mc - 2]] if mc >= 2 else [])
                    evs.append(ev)
                    sqs.append(sq)
                    slast = None
                    for j in range(2):
                        p.mm(pz6[j][:], ones_bf[:],
                             abf[mc][:, j * 512:(j + 1) * 512],
                             mc == 0, mc == NDC - 1, deps=[ev])
                        slast = p.mm(pq6[j][:], ones_bf[:],
                                     sqb[mc % 2][:, j * 512:(j + 1) * 512],
                                     mc == 0, mc == NDC - 1, deps=[sq])
                    stat6.append(slast)
                    drive(4)
                nr6 = ln_rows(rows6, [(pz6[j][:], j * 512) for j in range(2)],
                              [(pq6[j][:], j * 512) for j in range(2)],
                              1024, deps=[stat6[-1]])
                bc = bcast_rows(rows6, pwo[0], pwo[1], 1024,
                                deps=[nr6, evs[-1], sqs[-1]])
                x2_st = [None, None]
                for mc in range(NDC):
                    xld = p.dma("sync", xbuf[mc % 2][:],
                                xt[mc * 128:(mc + 1) * 128, c0:c0 + 1024],
                                deps=[x2_st[mc % 2]], slot=f"xb{mc % 2}")
                    t1 = p.tt(tbuf[mc % 2][:], abf[mc][:], pwo[0][:], ALU.mult,
                              deps=(bc if mc == 0 else []))
                    t2 = p.tt(tbuf[mc % 2][:], tbuf[mc % 2][:], pwo[1][:],
                              ALU.add, deps=[t1])
                    t3 = p.tt(x2b[mc % 2][:], tbuf[mc % 2][:], xbuf[mc % 2][:],
                              ALU.add, deps=[xld, x2_st[mc % 2], t2])
                    x2_st[mc % 2] = p.dma(
                        "sync", x2s[mc * 128:(mc + 1) * 128, c0:c0 + 1024],
                        x2b[mc % 2][:], deps=[t3], slot=f"x2{mc % 2}")
                attn_ev.extend(s for s in x2_st if s)
                if debug:
                    for mc in range(NDC):
                        p.dma("sync",
                              dbg["d_attn2"][mc * 128:(mc + 1) * 128,
                                             c0:c0 + 1024],
                              abf[mc][:], deps=[evs[mc]])
            p.barrier()

        drain_wstream()
        qscope.close()
        assert quant_done[0] is not None and quant_done[1] is not None
        half_guard = []
        p9o = contextlib.ExitStack()
        xbuf9 = [SB(p9o, f"xbuf9_{i}", [128, 1024], F32) for i in range(2)]
        tbuf9 = [SB(p9o, f"tbuf9_{i}", [128, 1024], F32) for i in range(2)]
        ob9 = [SB(p9o, f"ob9_{i}", [128, 1024], F32) for i in range(2)]
        o_st = [None, None]
        for hf in range(2):
            c0 = hf * 1024
            with contextlib.ExitStack() as ph:
                hid = SB(ph, f"hid{hf}", [128, NFH * 1024], BF16)
                rowsh = SB(ph, f"rowsh{hf}", [1, 5 * 1024], F32)
                hid_ev = []
                with contextlib.ExitStack() as psx:
                    xb2 = [SB(psx, f"xb2{hf}_{dc}", [128, 1024], BF16)
                           for dc in range(NDC)]
                    x2_ld = [p.dma("gpsimd", xb2[dc][:],
                                   x2s[dc * 128:(dc + 1) * 128, c0:c0 + 1024],
                                   deps=attn_ev + half_guard, slot="x2l")
                             for dc in range(NDC)]
                    with contextlib.ExitStack() as ps:
                        sq9 = [SB(ps, f"sq9{hf}_{i}", [128, 1024], BF16)
                               for i in range(2)]
                        tmp9 = [SB(ps, f"tmp9{hf}_{i}", [128, 1024], F32)
                                for i in range(2)]
                        pz9 = [PS(ps, f"pz9{hf}_{j}", [1, 512]) for j in range(2)]
                        pq9 = [PS(ps, f"pq9{hf}_{j}", [1, 512]) for j in range(2)]
                        pb9 = [PS(ps, f"pb9{hf}_{i}", [128, 1024]) for i in range(2)]
                        stat9 = []
                        for dc in range(NDC):
                            sq = p.act(sq9[dc % 2][:], xb2[dc][:], AF.Square,
                                       deps=[x2_ld[-1]] +
                                            ([stat9[dc - 2]] if dc >= 2 else []))
                            last = None
                            for j in range(2):
                                p.mm(pz9[j][:], ones_bf[:],
                                     xb2[dc][:, j * 512:(j + 1) * 512],
                                     dc == 0, dc == NDC - 1, deps=[sq])
                                last = p.mm(pq9[j][:], ones_bf[:],
                                            sq9[dc % 2][:, j * 512:(j + 1) * 512],
                                            dc == 0, dc == NDC - 1, deps=[sq])
                            stat9.append(last)
                        nr9 = ln_rows(rowsh, [(pz9[j][:], j * 512)
                                       for j in range(2)],
                                      [(pq9[j][:], j * 512)
                                       for j in range(2)],
                                      1024, deps=stat9)
                        bc9 = bcast_rows(rowsh, pb9[0], pb9[1], 1024, deps=[nr9])
                        xn2_ev = []
                        for dc in range(NDC):
                            t1 = p.tt(tmp9[dc % 2][:], xb2[dc][:], pb9[0][:],
                                      ALU.mult, deps=(bc9 if dc == 0 else []))
                            xn2_ev.append(p.tt(xb2[dc][:], tmp9[dc % 2][:],
                                               pb9[1][:], ALU.add, deps=[t1]))
                        if debug:
                            for dc in range(NDC):
                                p.dma("sync",
                                      dbg["d_xn2"][dc * 128:(dc + 1) * 128,
                                                   c0:c0 + 1024],
                                      xb2[dc][:], deps=[xn2_ev[dc]])
                    p.barrier()

                    with contextlib.ExitStack() as pw:
                        qw1b = [SB(pw, f"qw1b{hf}_{i}", [128, S], BF16)
                                for i in range(4)]
                        qw2b = [SB(pw, f"qw2b{hf}_{i}", [128, S], BF16)
                                for i in range(4)]
                        sigb = [SB(pw, f"sigb{hf}_{i}", [128, 1024], BF16)
                                for i in range(2)]
                        px1 = [PS(pw, f"px1{hf}_{i}", [128, 1024]) for i in range(2)]
                        px2 = [PS(pw, f"px2{hf}_{i}", [128, 1024]) for i in range(2)]
                        x2_mm = []
                        for hc in range(NFH):
                            ld1 = p.dma("sync",
                                        qw1b[hc % 4][:].rearrange(
                                            "p (a c) -> p a c", c=128),
                                        qw1s[hc],
                                        deps=quant_done +
                                             ([x2_mm[hc - 4]] if hc >= 4 else []),
                                        slot=f"w1{hc % 4}")
                            ld2 = p.dma("sync",
                                        qw2b[hc % 4][:].rearrange(
                                            "p (a c) -> p a c", c=128),
                                        qw2s[hc],
                                        deps=[x2_mm[hc - 4]] if hc >= 4 else [],
                                        slot=f"w2{hc % 4}")
                            m1 = m2 = None
                            for dc in range(NDC):
                                for j in range(2):
                                    m1 = p.mm(px1[hc % 2][:, j * 512:(j + 1) * 512],
                                              qw1b[hc % 4][:,
                                                           dc * 128:(dc + 1) * 128],
                                              xb2[dc][:, j * 512:(j + 1) * 512],
                                              dc == 0, dc == NDC - 1,
                                              deps=[ld1, xn2_ev[dc]] +
                                                   ([hid_ev[hc - 2]]
                                                    if hc >= 2 and dc == 0
                                                    else []))
                            for dc in range(NDC):
                                for j in range(2):
                                    m2 = p.mm(px2[hc % 2][:, j * 512:(j + 1) * 512],
                                              qw2b[hc % 4][:,
                                                           dc * 128:(dc + 1) * 128],
                                              xb2[dc][:, j * 512:(j + 1) * 512],
                                              dc == 0, dc == NDC - 1,
                                              deps=[ld2, xn2_ev[dc]])
                            x2_mm.append(m2)
                            sg = p.act(sigb[hc % 2][:], px1[hc % 2][:], AF.Sigmoid,
                                       deps=[m1, m2] +
                                            ([hid_ev[hc - 2]] if hc >= 2 else []))
                            hid_ev.append(
                                p.tt(hid[:, hc * 1024:(hc + 1) * 1024],
                                     sigb[hc % 2][:], px2[hc % 2][:], ALU.mult,
                                     deps=[sg]))
                        if debug and hf == 0:
                            for hc in range(NFH):
                                p.dma("sync",
                                      dbg["d_hid0"][:, hc * 1024:(hc + 1) * 1024],
                                      hid[:, hc * 1024:(hc + 1) * 1024],
                                      deps=[hid_ev[hc]])
                with contextlib.ExitStack() as pw:
                    qw3b = [SB(pw, f"qw3b{hf}_{i}", [128, FF], BF16) for i in range(2)]
                    fbf = [SB(pw, f"fbf{hf}_{dc}", [128, 1024], BF16)
                           for dc in range(NDC)]
                    sqf = [SB(pw, f"sqf{hf}_{i}", [128, 1024], BF16) for i in range(2)]
                    pw3 = [PS(pw, f"pw3{hf}_{i}", [128, 1024]) for i in range(2)]
                    pzf = [PS(pw, f"pzf{hf}_{j}", [1, 512]) for j in range(2)]
                    pqf = [PS(pw, f"pqf{hf}_{j}", [1, 512]) for j in range(2)]

                    f_ev, f_sq, statf = [], [], []
                    t3_last = None
                    for md in range(NDC):
                        ld = p.dma("sync",
                                   qw3b[md % 2][:].rearrange("p (a c) -> p a c",
                                                             c=128),
                                   qw3s[md],
                                   deps=quant_done +
                                        ([x2_mm[-1], hid_ev[-1]]
                                         if md < 2 else [f_ev[md - 2]]),
                                   slot=f"w3{md % 2}")
                        last = None
                        for hc in range(NFH):
                            for j in range(2):
                                last = p.mm(pw3[md % 2][:, j * 512:(j + 1) * 512],
                                            qw3b[md % 2][:,
                                                         hc * 128:(hc + 1) * 128],
                                            hid[:, hc * 1024 + j * 512:
                                                hc * 1024 + (j + 1) * 512],
                                            hc == 0, hc == NFH - 1,
                                            deps=[ld, hid_ev[hc]] +
                                                 ([f_sq[md - 2]]
                                                  if md >= 2 and hc == 0 else []))
                        ev = p.scopy(fbf[md][:], pw3[md % 2][:], deps=[last])
                        sq = p.act(sqf[md % 2][:], pw3[md % 2][:], AF.Square,
                                   deps=[statf[md - 2]] if md >= 2
                                        else [hid_ev[-1]])
                        f_ev.append(ev)
                        f_sq.append(sq)
                        if debug:
                            p.dma("sync",
                                  dbg["d_ffn"][md * 128:(md + 1) * 128,
                                               c0:c0 + 1024],
                                  fbf[md][:], deps=[ev], slot="dbg")
                        slast = None
                        for j in range(2):
                            p.mm(pzf[j][:], ones_bf[:],
                                 fbf[md][:, j * 512:(j + 1) * 512],
                                 md == 0, md == NDC - 1, deps=[ev])
                            slast = p.mm(pqf[j][:], ones_bf[:],
                                         sqf[md % 2][:, j * 512:(j + 1) * 512],
                                         md == 0, md == NDC - 1, deps=[sq])
                        statf.append(slast)
                    nrf = ln_rows(rowsh, [(pzf[j][:], j * 512) for j in range(2)],
                                  [(pqf[j][:], j * 512) for j in range(2)],
                                  1024, deps=[statf[-1]])
                    bcf = bcast_rows(rowsh, pw3[0], pw3[1], 1024,
                                     deps=[nrf, f_ev[-1], f_sq[-1]])
                    for md in range(NDC):
                        xld = p.dma("sync", xbuf9[md % 2][:],
                                    x2s[md * 128:(md + 1) * 128, c0:c0 + 1024],
                                    deps=attn_ev + [o_st[md % 2], hid_ev[-1]],
                                    slot=f"xb{md % 2}")
                        t1 = p.tt(tbuf9[md % 2][:], fbf[md][:], pw3[0][:],
                                  ALU.mult, deps=(bcf if md == 0 else []))
                        t2 = p.tt(tbuf9[md % 2][:], tbuf9[md % 2][:], pw3[1][:],
                                  ALU.add, deps=[t1])
                        t3 = p.tt(ob9[md % 2][:], tbuf9[md % 2][:],
                                  xbuf9[md % 2][:], ALU.add,
                                  deps=[xld, o_st[md % 2], t2])
                        t3_last = t3
                        o_st[md % 2] = p.dma(
                            "sync", outT[md * 128:(md + 1) * 128, c0:c0 + 1024],
                            ob9[md % 2][:], deps=[t3], slot=f"o{md % 2}")
                half_guard = [x2_mm[-1], hid_ev[-1], f_ev[-1], f_sq[-1],
                              nrf, statf[-1], t3_last]

        p9o.close()
        p.barrier()
        for s in sorted(set(_ENG_SEM.values())) + p.dma_sems():
            sems[s] = E(nc.semaphore(s))
        p.finish(block, sems)
    return nc, dbg


def _host_inputs(x, Wq, Wk, Wv, Wo, w1, w2, w3):
    qcols = []
    for t in range(16):
        for h in (EVENS[t], ODDS[t]):
            qcols += [64 * h + 2 * j for j in range(32)]
            qcols += [64 * h + 2 * j + 1 for j in range(32)]
    kcols = []
    for g in range(NKV):
        kcols += [64 * g + 2 * j for j in range(32)]
        kcols += [64 * g + 2 * j + 1 for j in range(32)]
    worows = []
    for t in range(16):
        for h in (EVENS[t], ODDS[t]):
            worows += [64 * h + j for j in range(64)]

    wq_p = np.ascontiguousarray(Wq[:, qcols])
    wk_p = np.ascontiguousarray(Wk[:, kcols])
    wo_p = np.ascontiguousarray(Wo[worows, :])
    wq4 = np.ascontiguousarray(
        wq_p.reshape(NDC, 128, NDC, 128).transpose(2, 1, 0, 3))
    wk4 = np.ascontiguousarray(
        wk_p.reshape(NDC, 128, 4, 128).transpose(2, 1, 0, 3))
    wo4 = np.ascontiguousarray(
        wo_p.reshape(NDC, 128, NDC, 128).transpose(2, 1, 0, 3))
    w14 = np.ascontiguousarray(
        w1.reshape(NFH, 128, NDC, 128).transpose(0, 3, 2, 1))
    w24 = np.ascontiguousarray(
        w2.reshape(NFH, 128, NDC, 128).transpose(0, 3, 2, 1))
    w34 = np.ascontiguousarray(
        w3.reshape(NDC, 128, NFH, 128).transpose(0, 3, 2, 1))

    half = 32
    inv = 1.0 / (10000.0 ** (np.arange(half, dtype=np.float64) / half))
    ang = np.outer(inv, np.arange(S, dtype=np.float64))
    s8 = 1.0 / np.sqrt(np.float64(8.0))
    cosf = np.cos(ang) * s8
    sinf = np.sin(ang) * s8
    tabA = np.ascontiguousarray(np.tile(cosf, (4, 1)).astype(np.float32))
    tabB = np.ascontiguousarray(
        np.tile(np.concatenate([-sinf, sinf], axis=0), (2, 1)).astype(np.float32))

    ii, jj = np.meshgrid(np.arange(128), np.arange(128), indexing="ij")
    dmsk = np.ascontiguousarray((jj >= ii).astype(ml_dtypes.bfloat16))

    shared = {"wq4": wq4, "wk4": wk4, "wv2": np.ascontiguousarray(Wv),
              "wo4": wo4, "w14": w14, "w24": w24, "w34": w34,
              "tabA": tabA, "tabB": tabB, "dmask": dmsk}
    return [dict(shared, xt=np.ascontiguousarray(x[b].T)) for b in range(8)]


def kernel(L0_attn_x, L0_attn_Wq, L0_attn_Wk, L0_attn_Wv, L0_attn_Wo,
           L0_ffn_w1, L0_ffn_w2, L0_ffn_w3):
    x = np.asarray(L0_attn_x, dtype=np.float32)
    in_maps = _host_inputs(x, np.asarray(L0_attn_Wq, dtype=np.float32),
                           np.asarray(L0_attn_Wk, dtype=np.float32),
                           np.asarray(L0_attn_Wv, dtype=np.float32),
                           np.asarray(L0_attn_Wo, dtype=np.float32),
                           np.asarray(L0_ffn_w1, dtype=np.float32),
                           np.asarray(L0_ffn_w2, dtype=np.float32),
                           np.asarray(L0_ffn_w3, dtype=np.float32))
    nc, _ = build(debug=False)
    res = run_bass_kernel_spmd(nc, in_maps, core_ids=list(range(8)))
    out = np.stack([np.asarray(r["outT"], dtype=np.float32).T
                    for r in res.results], axis=0)
    return np.ascontiguousarray(out)


# version 16
# speedup vs baseline: 1.0157x; 1.0157x over previous
import contextlib
import numpy as np
import ml_dtypes
import concourse.bass as bass
import concourse.mybir as mybir
from concourse.bass_utils import run_bass_kernel_spmd

F32 = mybir.dt.float32
BF16 = mybir.dt.bfloat16
AF = mybir.ActivationFunctionType
ALU = mybir.AluOpType
AX = mybir.AxisListType

D = 2048
S = 2048
NH = 32
NKV = 8
FF = 5632
NDC = D // 128
NFH = FF // 128
NJB = S // 512
EPS_LN = 1e-5
EPS_Q = 1e-8

EVENS = [0, 1, 2, 3, 8, 9, 10, 11, 16, 17, 18, 19, 24, 25, 26, 27]
ODDS = [4, 5, 6, 7, 12, 13, 14, 15, 20, 21, 22, 23, 28, 29, 30, 31]

_ENG_SEM = {"tensor": "sT", "vector": "sV", "scalar": "sS",
            "sync": "dS", "gpsimd": "dG"}


class Op:
    __slots__ = ("eng", "emit", "deps", "dma", "sem", "val", "inc_k", "waits",
                 "fsz")

    def __init__(self, eng, emit, deps, dma, sem=None, fsz=0):
        self.eng = eng
        self.emit = emit
        self.deps = [d for d in deps if d is not None]
        self.dma = dma
        self.sem = sem or _ENG_SEM[eng]
        self.val = 0
        self.inc_k = 0
        self.waits = []
        self.fsz = fsz


class PB:

    def __init__(self):
        self.ops = []

    def add(self, eng, emit, deps=(), dma=False, sem=None, out=None):
        fsz = 0
        if out is not None:
            try:
                fsz = int(out.free_size())
            except TypeError:
                fsz = int(out.free_size)
        op = Op(eng, emit, deps, dma, sem=sem, fsz=fsz)
        self.ops.append(op)
        return op

    def dma(self, eng, out, in_, deps=(), slot="d", accum=None):
        if accum is None:
            emit = lambda e, o=out, i=in_: e.dma_start(out=o, in_=i)
        else:
            emit = lambda e, o=out, i=in_, a=accum: e.dma_start(out=o, in_=i,
                                                               accum_op=a)
        return self.add(eng, emit, deps, dma=True, sem=f"D_{eng[:2]}_{slot}")

    def mm(self, out, lhsT, rhs, start, stop, deps=(), tp=None):
        return self.add("tensor",
                        lambda e, o=out, l=lhsT, r=rhs, st=start, sp=stop, tp=tp:
                        e.matmul(o, l, r, start=st, stop=sp, tile_position=tp), deps)

    def act(self, out, in_, func, deps=(), bias=..., scale=...):
        def f(e, o=out, i=in_, fn=func, b=bias, s=scale):
            kw = {}
            if b is not ...:
                kw["bias"] = b
            if s is not ...:
                kw["scale"] = s
            return e.activation(o, i, fn, **kw)
        return self.add("scalar", f, deps, out=out)

    def scopy(self, out, in_, deps=()):
        return self.add("scalar", lambda e, o=out, i=in_: e.copy(o, i), deps,
                        out=out)

    def tt(self, out, a, b, opk, deps=()):
        return self.add("vector",
                        lambda e, o=out, x=a, y=b, k=opk: e.tensor_tensor(o, x, y, k),
                        deps, out=out)

    def ts(self, out, in0, s1, s2, op0, op1, deps=()):
        return self.add("vector",
                        lambda e, o=out, i=in0, a=s1, b=s2, o0=op0, o1=op1:
                        e.tensor_scalar(o, i, a, b, o0, o1), deps, out=out)

    def stt(self, out, in0, sc, in1, op0, op1, deps=()):
        return self.add("vector",
                        lambda e, o=out, i=in0, s=sc, j=in1, o0=op0, o1=op1:
                        e.scalar_tensor_tensor(o, i, s, j, o0, o1), deps, out=out)

    def vcopy(self, out, in_, deps=()):
        return self.add("vector", lambda e, o=out, i=in_: e.tensor_copy(o, i),
                        deps, out=out)

    def red(self, out, in_, opk, deps=(), absval=None):
        return self.add("vector",
                        lambda e, o=out, i=in_, k=opk, av=absval:
                        e.tensor_reduce(o, i, AX.X, k, apply_absolute_value=av),
                        deps, out=out)

    def memset(self, ap, val, deps=()):
        return self.add("vector", lambda e, a=ap, v=val: e.memset(a, v), deps)

    def barrier(self):
        last = {}
        for op in self.ops:
            if op.emit is not None:
                last[op.sem] = op
        for eng in _ENG_SEM:
            deps = [o for sm, o in last.items() if o.eng != eng or o.dma]
            self.add(eng, None, deps)

    def dma_sems(self):
        return sorted({op.sem for op in self.ops if op.dma})

    def finish(self, block, sems):
        for op in self.ops:
            for d in op.deps:
                if d.inc_k == 0 and not d.dma:
                    d.inc_k = 1
        cnt = {s: 0 for s in set(_ENG_SEM.values())}
        for op in self.ops:
            cnt.setdefault(op.sem, 0)
        for op in self.ops:
            if op.dma:
                op.inc_k = 16
            if op.inc_k:
                cnt[op.sem] += op.inc_k
                op.val = cnt[op.sem]
        waited = {eng: {} for eng in _ENG_SEM}
        for op in self.ops:
            w = waited[op.eng]
            for d in op.deps:
                if d.eng == op.eng and not d.dma and (
                        d.eng == "tensor" or
                        (d.fsz >= 1024 and op.fsz >= 1024)):
                    continue
                assert d.inc_k > 0 and d.val > 0, "dep without inc"
                if w.get(d.sem, 0) < d.val:
                    op.waits.append((d.sem, d.val))
                    w[d.sem] = d.val
        for eng_name in _ENG_SEM:
            grp = [op for op in self.ops if op.eng == eng_name]
            if not grp:
                continue

            def body(eng, grp=grp):
                for op in grp:
                    for s, v in op.waits:
                        eng.wait_ge(sems[s], v)
                    if op.emit is None:
                        continue
                    ins = op.emit(eng)
                    if op.inc_k:
                        ins.then_inc(sems[op.sem], op.inc_k)
            getattr(block, eng_name)(body)


def build(debug=False):
    nc = bass.Bass("TRN2", target_bir_lowering=False)
    dt = nc.dram_tensor

    xt = dt("xt", [D, S], F32, kind="ExternalInput").ap()
    wq4 = dt("wq4", [NDC, 128, NDC, 128], F32, kind="ExternalInput").ap()
    wk4 = dt("wk4", [4, 128, NDC, 128], F32, kind="ExternalInput").ap()
    wv2 = dt("wv2", [D, 512], F32, kind="ExternalInput").ap()
    wo4 = dt("wo4", [NDC, 128, NDC, 128], F32, kind="ExternalInput").ap()
    w14 = dt("w14", [NFH, 128, NDC, 128], F32, kind="ExternalInput").ap()
    w24 = dt("w24", [NFH, 128, NDC, 128], F32, kind="ExternalInput").ap()
    w34 = dt("w34", [NDC, 128, NFH, 128], F32, kind="ExternalInput").ap()
    tabA = dt("tabA", [128, S], F32, kind="ExternalInput").ap()
    tabB = dt("tabB", [128, S], F32, kind="ExternalInput").ap()
    dmask = dt("dmask", [128, 128], BF16, kind="ExternalInput").ap()

    outT = dt("outT", [D, S], F32, kind="ExternalOutput").ap()
    x2s = dt("x2s", [D, S], F32, kind="Internal").ap()
    qs = dt("qs", [NJB, NDC, 128, 512], BF16, kind="Internal").ap()
    ys = dt("ys", [NDC, 128, S], BF16, kind="Internal").ap()
    qw1s = dt("qw1s", [NFH, 128, NDC, 128], BF16, kind="Internal").ap()
    qw2s = dt("qw2s", [NFH, 128, NDC, 128], BF16, kind="Internal").ap()
    qw3s = dt("qw3s", [NDC, 128, NFH, 128], BF16, kind="Internal").ap()
    gsums = dt("gsums", [3, 128, 1], F32, kind="Internal").ap()
    thrs = dt("thrs", [3, 1, 1], F32, kind="Internal").ap()

    dbg = {}
    if debug:
        for nm, shape, ddt in [
            ("d_xn", [D, S], BF16), ("d_kb", [512, S], BF16),
            ("d_vb", [128, NKV * 16 * 65], BF16), ("d_thr", [128, 3], F32),
            ("d_bias", [128, 16], F32), ("d_attn2", [D, S], BF16),
            ("d_kn", [1, 256], F32), ("d_xn2", [D, S], BF16),
            ("d_yh", [D, S], BF16),
            ("d_hid0", [128, NFH * 1024], BF16), ("d_ffn", [D, S], BF16),
        ]:
            dbg[nm] = dt(nm, shape, ddt, kind="ExternalOutput").ap()

    p = PB()
    es = contextlib.ExitStack()
    with es:
        E = es.enter_context
        block = E(nc.Block())
        sems = {}

        def SB(st, n, sh, d):
            return st.enter_context(nc.sbuf_tensor(n, sh, d))

        def PS(st, n, sh):
            return st.enter_context(nc.psum_tensor(n, sh, F32))

        ones_bf = SB(es, "ones_bf", [128, 1], BF16)
        ones_fr = SB(es, "ones_fr", [1, 128], F32)
        ones_fc = SB(es, "ones_fc", [128, 1], F32)
        onesr64 = SB(es, "onesr64", [65, 128], BF16)
        dmk = SB(es, "dmk", [128, 128], BF16)
        krows = SB(es, "krows", [1, 256], F32)
        srow = SB(es, "srow", [1, 96], F32)
        bias_all = SB(es, "bias_all", [128, 16], F32)
        thrP = SB(es, "thrP", [128, 3], F32)
        thrN = SB(es, "thrN", [128, 3], F32)
        lnr = SB(es, "lnr", [65, 512], F32)
        epsln = SB(es, "epsln", [1, 1], F32)
        recb = SB(es, "recb", [65, 512], BF16)
        KT, KN2, KREP, QM4, QM2 = 0, 32, 64, 96, 224

        ld_mask = p.dma("sync", dmk[:], dmask, slot="msk")
        p.memset(ones_bf[:], 1.0)
        p.memset(ones_fr[:], 1.0)
        p.memset(ones_fc[:], 1.0)
        p.memset(onesr64[:], 0.0)
        c_ones64 = p.memset(onesr64[64:65, :], 1.0)
        c_eps = p.memset(epsln[:], EPS_LN)

        W_SLICES = []
        for wap, qap in [(w14, qw1s), (w24, qw2s)]:
            sl = []
            for i in range(NFH):
                for h in range(2):
                    sl.append((wap[i, :, h * 8:(h + 1) * 8, :]
                               .rearrange("p a c -> p (a c)"),
                               qap[i, :, h * 8:(h + 1) * 8, :]
                               .rearrange("p a c -> p (a c)"), 1024))
            W_SLICES.append(sl)
        sl3 = []
        for i in range(NDC):
            for h in range(4):
                sl3.append(
                    (w34[i, :, h * 11:(h + 1) * 11, :].rearrange("p a c -> p (a c)"),
                     qw3s[i, :, h * 11:(h + 1) * 11, :].rearrange("p a c -> p (a c)"),
                     11 * 128))
        W_SLICES.append(sl3)

        qscope = contextlib.ExitStack()

        def SBR(st, n, sh, d):
            return st.enter_context(nc.sbuf_tensor(n, sh, d, side="right"))
        gbuf = [SBR(qscope, f"gbuf{i}", [128, 1408], F32) for i in range(3)]
        gcol = SBR(qscope, "gcol", [128, 96], F32)
        g128 = SBR(qscope, "g128", [128, 1], F32)
        grow = SBR(qscope, "grow", [1, 1], F32)
        growr = SBR(qscope, "growr", [1, 128], F32)
        qtmp = SBR(qscope, "qtmp", [128, 1408], F32)
        qout = [SBR(qscope, f"qout{i}", [128, 1408], BF16) for i in range(2)]

        quant_done = [None, None]
        _qd_state = {"done": False}

        def wstream_gen():
            thr_ev = []
            for wi, slices in enumerate(W_SLICES):
                reds = []
                lds = []
                for i in range(len(slices) + 2):
                    if i < len(slices):
                        srcw, _q, wid = slices[i]
                        lds.append(p.dma("gpsimd", gbuf[i % 3][:, 0:wid], srcw,
                                         deps=[reds[i - 3]] if i >= 3 else [],
                                         slot=f"g{i % 3}"))
                    j = i - 2
                    if j >= 0:
                        wid = slices[j][2]
                        reds.append(p.red(gcol[:, j:j + 1],
                                          gbuf[j % 3][:, 0:wid], ALU.add,
                                          deps=[lds[j]], absval=True))
                    yield
                rsum = p.red(g128[:], gcol[:, 0:len(slices)], ALU.add, deps=reds)
                st1 = p.dma("gpsimd", gsums[wi], g128[:], deps=[rsum],
                            slot="thr")
                ldr = p.dma("gpsimd", growr[:],
                            gsums[wi].rearrange("p x -> x (p)"), deps=[st1],
                            slot="thr")
                r2 = p.red(grow[:], growr[:], ALU.add, deps=[ldr])
                t1 = p.ts(grow[:], grow[:], 0.5 / (FF * D), 0.5 * EPS_Q,
                          ALU.mult, ALU.add, deps=[r2])
                st2 = p.dma("gpsimd", thrs[wi], grow[:], deps=[t1],
                            slot="thr")
                bcl = p.dma("gpsimd", thrP[:, wi:wi + 1],
                            thrs[wi].broadcast_to([128, 1]), deps=[st2],
                            slot="thr")
                tn = p.ts(thrN[:, wi:wi + 1], thrP[:, wi:wi + 1], -1.0, None,
                          ALU.mult, ALU.bypass, deps=[bcl])
                thr_ev.append((bcl, tn))
                yield
            prev_store = [None, None]
            prev_pass2 = [None, None, None]
            ALLQ = [(wi, srcw, qdst, wid) for wi, slices in enumerate(W_SLICES)
                    for (srcw, qdst, wid) in slices]
            lds = []
            last_st = None
            for i in range(len(ALLQ) + 2):
                if i < len(ALLQ):
                    wi, srcw, qdst, wid = ALLQ[i]
                    lds.append(p.dma("gpsimd", gbuf[i % 3][:, 0:wid], srcw,
                                     deps=[prev_pass2[i % 3]],
                                     slot=f"g{i % 3}"))
                j = i - 2
                if j >= 0:
                    wi, srcw, qdst, wid = ALLQ[j]
                    p1 = p.ts(qtmp[:, 0:wid], gbuf[j % 3][:, 0:wid],
                              thrP[:, wi:wi + 1], 1.0, ALU.is_gt, ALU.subtract,
                              deps=[lds[j], thr_ev[wi][0], thr_ev[wi][1],
                                    prev_store[j % 2]])
                    p2 = p.stt(qout[j % 2][:, 0:wid], gbuf[j % 3][:, 0:wid],
                               thrN[:, wi:wi + 1], qtmp[:, 0:wid],
                               ALU.is_gt, ALU.add, deps=[p1])
                    st = p.dma("gpsimd", qdst, qout[j % 2][:, 0:wid],
                               deps=[p2], slot=f"q{j % 2}")
                    prev_store[j % 2] = st
                    prev_pass2[j % 3] = p2
                    last_st = st
                yield
            quant_done[0] = last_st
            quant_done[1] = prev_store[(len(ALLQ) - 2) % 2]
            _qd_state["done"] = True

        _wgen = wstream_gen()

        def drive(n=1):
            for _ in range(n):
                if next(_wgen, "END") == "END":
                    break

        def drain_wstream():
            while not _qd_state["done"]:
                if next(_wgen, "END") == "END":
                    break


        def ln_rows(rows, pz, pq2, width, deps):
            RM, RX2, RMS, RR = 0, width, 2 * width, 3 * width
            RV = RX2
            RNR = 4 * width
            vops = []
            for ap_, c in pz:
                vops.append(p.ts(rows[0:1, RM + c:RM + c + 512], ap_, 1.0 / D, None,
                                 ALU.mult, ALU.bypass, deps=deps))
            for ap_, c in pq2:
                vops.append(p.ts(rows[0:1, RX2 + c:RX2 + c + 512], ap_, 1.0 / D,
                                 None, ALU.mult, ALU.bypass, deps=deps))
            m_ = rows[0:1, RM:RM + width]
            msq = p.tt(rows[0:1, RMS:RMS + width], m_, m_, ALU.mult, deps=vops)
            vv = p.tt(rows[0:1, RV:RV + width], rows[0:1, RX2:RX2 + width],
                      rows[0:1, RMS:RMS + width], ALU.subtract, deps=[msq])
            a1 = p.act(rows[0:1, RMS:RMS + width], rows[0:1, RV:RV + width],
                       AF.Ln, deps=[vv, c_eps], bias=epsln[:])
            a2 = p.act(rows[0:1, RR:RR + width], rows[0:1, RMS:RMS + width],
                       AF.Exp, deps=[a1], scale=-0.5)
            return p.stt(rows[0:1, RNR:RNR + width], rows[0:1, RM:RM + width],
                         -1.0, rows[0:1, RR:RR + width], ALU.mult, ALU.mult,
                         deps=[a2])

        def bcast_rows(rows, pbr, pbn, width, deps):
            RR, RNR = 3 * width, 4 * width
            outs = []
            for j in range(width // 512):
                c = j * 512
                outs.append(p.mm(pbr[:, c:c + 512], ones_fr[:],
                                 rows[0:1, RR + c:RR + c + 512], True, True,
                                 deps=deps))
                outs.append(p.mm(pbn[:, c:c + 512], ones_fr[:],
                                 rows[0:1, RNR + c:RNR + c + 512], True, True,
                                 deps=deps))
            return outs

        with contextlib.ExitStack() as att:
            kb = [SB(att, f"kb{m}", [128, S], BF16) for m in range(4)]
            vb = [SB(att, f"vb{g}", [128, 16 * 65], BF16) for g in range(NKV)]
            vb_init = [p.memset(vb[g][:], 1.0) for g in range(NKV)]

            with contextlib.ExitStack() as lB:
                xn = [SB(lB, f"xn{i}", [128, S], BF16) for i in range(NDC)]
                tA = SB(lB, "tA", [128, S], F32)
                tB = SB(lB, "tB", [128, S], F32)
                ld_tA = p.dma("sync", tA[:], tabA, slot="tA")
                ld_tB = p.dma("sync", tB[:], tabB, slot="tB")

                p12 = contextlib.ExitStack()
                rows1 = SB(p12, "rows1", [1, 5 * S], F32)
                with contextlib.ExitStack() as ps:
                    xsq = [SB(ps, f"xsq{i}", [128, S], BF16) for i in range(2)]
                    pz = [PS(ps, f"pz{j}", [1, 512]) for j in range(NJB)]
                    pq2 = [PS(ps, f"pq2{j}", [1, 512]) for j in range(NJB)]
                    x_ld = [p.dma("gpsimd", xn[dc][:],
                                  xt[dc * 128:(dc + 1) * 128, :], slot="xn")
                            for dc in range(NDC)]
                    stat = []
                    for dc in range(NDC):
                        sq = p.act(xsq[dc % 2][:], xn[dc][:], AF.Square,
                                   deps=[x_ld[-1]] +
                                        ([stat[dc - 2]] if dc >= 2 else []))
                        last = None
                        for j in range(NJB):
                            p.mm(pz[j][:], ones_bf[:],
                                 xn[dc][:, j * 512:(j + 1) * 512],
                                 dc == 0, dc == NDC - 1,
                                 deps=[sq] + ([c_eps] if dc == 0 else []))
                            last = p.mm(pq2[j][:], ones_bf[:],
                                        xsq[dc % 2][:, j * 512:(j + 1) * 512],
                                        dc == 0, dc == NDC - 1, deps=[sq])
                        stat.append(last)
                    nr1 = ln_rows(rows1, [(pz[j][:], j * 512) for j in range(NJB)],
                                  [(pq2[j][:], j * 512) for j in range(NJB)],
                                  S, deps=stat)

                with contextlib.ExitStack() as ps:
                    pbr = PS(ps, "pbr", [128, S])
                    pbn = PS(ps, "pbn", [128, S])
                    tmpf = [SB(ps, f"tmpf{i}", [128, S], F32) for i in range(2)]
                    bc = bcast_rows(rows1, pbr, pbn, S, deps=[nr1])
                    xn_ev = []
                    for dc in range(NDC):
                        t1 = p.tt(tmpf[dc % 2][:], xn[dc][:], pbr[:], ALU.mult,
                                  deps=bc if dc == 0 else [])
                        xn_ev.append(p.tt(xn[dc][:], tmpf[dc % 2][:], pbn[:],
                                          ALU.add, deps=[t1]))
                        drive(2)
                p12.close()
                if debug:
                    for dc in range(NDC):
                        p.dma("sync", dbg["d_xn"][dc * 128:(dc + 1) * 128, :],
                              xn[dc][:], deps=[xn_ev[dc]], slot="dbg")
                p.barrier()

                with contextlib.ExitStack() as kns:
                    pkn = [PS(kns, f"pkn{i}", [1, 512]) for i in range(2)]

                    with contextlib.ExitStack() as ps:
                        wkb = [SB(ps, f"wkb{i}", [128, S], BF16) for i in range(2)]
                        wvb = SB(ps, "wvb", [128, NDC * 512], BF16)
                        kraw = [SB(ps, f"kraw{i}", [128, 1024], F32)
                                for i in range(2)]
                        kswap = SB(ps, "kswap", [128, 1024], F32)
                        ksq = SB(ps, "ksq", [128, S], BF16)
                        pkv = [PS(ps, f"pkv{i}", [128, 1024]) for i in range(2)]
                        pv = [PS(ps, f"pv{i}", [128, 512]) for i in range(2)]

                        k_mm, k_evac, k_swap, k_rope = [], [], [], []
                        for m in range(4):
                            ld = p.dma("gpsimd",
                                       wkb[m % 2][:].rearrange("p (a c) -> p a c",
                                                               c=128),
                                       wk4[m],
                                       deps=[k_mm[2 * m - 3]] if m >= 2 else [],
                                       slot=f"wk{m % 2}")
                            for hf in range(2):
                                step = 2 * m + hf
                                last = None
                                for dc in range(NDC):
                                    for j in range(2):
                                        c = hf * 1024 + j * 512
                                        last = p.mm(
                                            pkv[step % 2][:, j * 512:(j + 1) * 512],
                                            wkb[m % 2][:, dc * 128:(dc + 1) * 128],
                                            xn[dc][:, c:c + 512],
                                            dc == 0, dc == NDC - 1,
                                            deps=[ld, xn_ev[dc]] +
                                                 ([k_evac[step - 2]]
                                                  if step >= 2 and dc == 0 else []))
                                k_mm.append(last)
                                ev = p.scopy(kraw[step % 2][:], pkv[step % 2][:],
                                             deps=[last] +
                                                  ([k_swap[step - 2]]
                                                   if step >= 2 else []) +
                                                  ([k_rope[step - 2]]
                                                   if step >= 2 else []))
                                k_evac.append(ev)
                                sw = None
                                for a in range(4):
                                    sw = p.dma(
                                        "gpsimd",
                                        kswap[(a ^ 1) * 32:(a ^ 1) * 32 + 32, :],
                                        kraw[step % 2][a * 32:a * 32 + 32, :],
                                        deps=[ev] + ([k_rope[step - 1]]
                                                     if step >= 1 else []),
                                        slot="swp")
                                k_swap.append(sw)
                                c = hf * 1024
                                tsw = p.tt(kswap[:], kswap[:], tB[:, c:c + 1024],
                                           ALU.mult, deps=[sw, ld_tB])
                                t2 = p.tt(kraw[step % 2][:], kraw[step % 2][:],
                                          tA[:, c:c + 1024], ALU.mult,
                                          deps=[ld_tA])
                                k_rope.append(
                                    p.tt(kb[m][:, c:c + 1024], kraw[step % 2][:],
                                         kswap[:], ALU.add, deps=[t2, tsw]))
                                drive(2)

                        kn_red = []
                        kn_mm_last = None
                        for m in range(4):
                            sq = p.act(ksq[:], kb[m][:], AF.Square,
                                       deps=[k_rope[2 * m], k_rope[2 * m + 1]] +
                                            ([kn_mm_last] if kn_mm_last else []))
                            for gh in range(2):
                                g = 2 * m + gh
                                p0 = 64 * gh
                                for j in range(NJB):
                                    slot = (g * NJB + j) % 2
                                    mm = p.mm(pkn[slot][:], ones_bf[p0:p0 + 64, :],
                                              ksq[p0:p0 + 64,
                                                  j * 512:(j + 1) * 512],
                                              True, True,
                                              deps=[sq] +
                                                   ([kn_red[-2]]
                                                    if len(kn_red) >= 2 else []))
                                    kn_mm_last = mm
                                    kn_red.append(p.red(
                                        krows[0:1, KT + 4 * g + j:
                                              KT + 4 * g + j + 1],
                                        pkn[slot][:], ALU.max, deps=[mm]))
                        kn2_ops = [p.red(krows[0:1, KN2 + g:KN2 + g + 1],
                                         krows[0:1, KT + 4 * g:KT + 4 * g + 4],
                                         ALU.max, deps=[kn_red[-1]])
                                   for g in range(NKV)]
                        src = krows[0:1, KN2:KN2 + 8].rearrange(
                            "r (x c) -> r x c", c=2)
                        src = src.unsqueeze(2).broadcast_to([1, 4, 4, 2])
                        knrep = p.vcopy(
                            krows[0:1, KREP:KREP + 32].rearrange(
                                "r (x b c) -> r x b c", b=4, c=2), src,
                            deps=kn2_ops)

                        ld_wv = p.dma("gpsimd",
                                      wvb[:].rearrange("p (a c) -> p a c", c=512),
                                      wv2.rearrange("(a p) c -> p a c", p=128),
                                      slot="wv")
                        v_ev = []
                        for tc in range(16):
                            last = None
                            for dc in range(NDC):
                                last = p.mm(pv[tc % 2][:],
                                            xn[dc][:, tc * 128:(tc + 1) * 128],
                                            wvb[:, dc * 512:(dc + 1) * 512],
                                            dc == 0, dc == NDC - 1,
                                            deps=[ld_wv, xn_ev[dc]] +
                                                 ([v_ev[tc - 2]]
                                                  if tc >= 2 and dc == 0 else []))
                            ev = None
                            for g in range(NKV):
                                ev = p.scopy(vb[g][:, tc * 65:tc * 65 + 64],
                                             pv[tc % 2][:, g * 64:(g + 1) * 64],
                                             deps=[last] +
                                                  (vb_init if tc == 0 else []))
                            v_ev.append(ev)
                            drive(1)

                        if debug:
                            for m in range(4):
                                p.dma("sync",
                                      dbg["d_kb"][m * 128:(m + 1) * 128, :],
                                      kb[m][:], deps=[k_rope[2 * m + 1]])
                            for g in range(NKV):
                                p.dma("sync",
                                      dbg["d_vb"][:, g * 1040:(g + 1) * 1040],
                                      vb[g][:], deps=[v_ev[-1]])
                    p.barrier()

                    with contextlib.ExitStack() as ps:
                        wqb = [SB(ps, f"wqb{i}", [128, S], BF16) for i in range(2)]
                        qraw = [SB(ps, f"qraw{i}", [128, S], F32) for i in range(2)]
                        qswap = SB(ps, "qswap", [128, S], F32)
                        qrot = [SB(ps, f"qrot{i}", [128, S], BF16)
                                for i in range(2)]
                        qsq = SB(ps, "qsq", [128, S], BF16)
                        pq = [PS(ps, f"pq{i}", [128, 1024]) for i in range(2)]

                        q_mmL, q_evac, q_swap, q_rope, q_store, qn_red, q_sq = \
                            [], [], [], [], [], [], []
                        for mc in range(NDC):
                            ld = p.dma("gpsimd",
                                       wqb[mc % 2][:].rearrange(
                                           "p (a c) -> p a c", c=128),
                                       wq4[mc],
                                       deps=[q_mmL[mc - 2]] if mc >= 2 else [],
                                       slot=f"wq{mc % 2}")
                            for hf in range(2):
                                step = 2 * mc + hf
                                last = None
                                for dc in range(NDC):
                                    for j in range(2):
                                        c = hf * 1024 + j * 512
                                        last = p.mm(
                                            pq[step % 2][:, j * 512:(j + 1) * 512],
                                            wqb[mc % 2][:, dc * 128:(dc + 1) * 128],
                                            xn[dc][:, c:c + 512],
                                            dc == 0, dc == NDC - 1,
                                            deps=[ld, xn_ev[dc]] +
                                                 ([q_evac[step - 2]]
                                                  if step >= 2 and dc == 0 else []))
                                ev = p.scopy(qraw[mc % 2][:, hf * 1024:
                                                          hf * 1024 + 1024],
                                             pq[step % 2][:],
                                             deps=[last] +
                                                  ([q_swap[mc - 2],
                                                    q_rope[mc - 2]]
                                                   if mc >= 2 and hf == 0 else []))
                                q_evac.append(ev)
                            q_mmL.append(last)
                            sw = None
                            for a in range(4):
                                sw = p.dma("gpsimd",
                                           qswap[(a ^ 1) * 32:(a ^ 1) * 32 + 32, :],
                                           qraw[mc % 2][a * 32:a * 32 + 32, :],
                                           deps=[q_evac[-1], q_evac[-2]] +
                                                ([q_rope[mc - 1]]
                                                 if mc >= 1 else []),
                                           slot="swp")
                            q_swap.append(sw)
                            tsw = p.tt(qswap[:], qswap[:], tB[:], ALU.mult,
                                       deps=[sw, ld_tB])
                            t2 = p.tt(qraw[mc % 2][:], qraw[mc % 2][:], tA[:],
                                      ALU.mult, deps=[ld_tA])
                            qr = p.tt(qrot[mc % 2][:], qraw[mc % 2][:], qswap[:],
                                      ALU.add,
                                      deps=[t2, tsw] +
                                           ([q_store[mc - 2], q_sq[mc - 2]]
                                            if mc >= 2 else []))
                            q_rope.append(qr)
                            st = None
                            for qb in range(NJB):
                                st = p.dma("sync", qs[qb, mc],
                                           qrot[mc % 2][:, qb * 512:(qb + 1) * 512],
                                           deps=[qr], slot="qst")
                            q_store.append(st)
                            sq = p.act(qsq[:], qrot[mc % 2][:], AF.Square,
                                       deps=[qr] +
                                            ([qn_red[-1]] if qn_red else []))
                            q_sq.append(sq)
                            drive(2)
                            for hh in range(2):
                                pos = 2 * mc + hh
                                for j in range(NJB):
                                    slot = (pos * NJB + j) % 2
                                    mm = p.mm(pkn[slot][:],
                                              ones_bf[hh * 64:hh * 64 + 64, :],
                                              qsq[hh * 64:hh * 64 + 64,
                                                  j * 512:(j + 1) * 512],
                                              True, True,
                                              deps=[sq, knrep] +
                                                   ([qn_red[-2]]
                                                    if len(qn_red) >= 2 else []))
                                    qn_red.append(p.red(
                                        krows[0:1, QM4 + 4 * pos + j:
                                              QM4 + 4 * pos + j + 1],
                                        pkn[slot][:], ALU.max, deps=[mm]))

            qm2_last = None
            for pos in range(NH):
                qm2_last = p.red(krows[0:1, QM2 + pos:QM2 + pos + 1],
                                 krows[0:1, QM4 + 4 * pos:QM4 + 4 * pos + 4],
                                 ALU.max, deps=[qn_red[-1]])
            s_a = p.ts(srow[0:1, 0:32], krows[0:1, QM2:QM2 + 32], 0.5, None,
                       ALU.mult, ALU.bypass, deps=[qm2_last])
            s_b = p.stt(srow[0:1, 32:64], krows[0:1, KREP:KREP + 32], 0.5,
                        srow[0:1, 0:32], ALU.mult, ALU.add,
                        deps=[s_a, knrep])
            s_c = p.tt(srow[0:1, 64:80], srow[0:1, 32:64:2],
                       srow[0:1, 33:64:2], ALU.max, deps=[s_b])
            sb16 = p.ts(srow[0:1, 80:96], srow[0:1, 64:80], -1.0, None,
                        ALU.mult, ALU.bypass, deps=[s_c])
            with contextlib.ExitStack() as ps:
                pbias = PS(ps, "pbias", [128, 16])
                mmb = p.mm(pbias[:], ones_fr[:], srow[0:1, 80:96], True, True,
                           deps=[sb16])
                bias_ev = p.vcopy(bias_all[:], pbias[:], deps=[mmb])
            if debug:
                p.dma("sync", dbg["d_bias"], bias_all[:], deps=[bias_ev], slot="dbg")
                p.dma("sync", dbg["d_kn"], krows[0:1, 0:256], deps=[bias_ev], slot="dbg")
            p.barrier()

            with contextlib.ExitStack() as ps:
                qsl = [SB(ps, f"qsl{t}", [128, 512], BF16) for t in range(NDC)]
                Phat = [SB(ps, f"Phat{i}", [128, 1024], BF16) for i in range(3)]
                oubT = [SB(ps, f"oubT{t}", [128, 512], BF16) for t in range(NDC)]
                ybuf = [SB(ps, f"ybuf{i}", [128, 512], BF16) for i in range(2)]
                srows = SB(ps, "srows", [65, 32 * 512], F32)
                recb = SB(ps, "recb2", [65, 1024], BF16)
                pS = [PS(ps, f"pS{i}", [128, 1024]) for i in range(2)]
                ouA = [PS(ps, f"ouA{i}", [65, 512]) for i in range(2)]
                ouB = [PS(ps, f"ouB{i}", [65, 512]) for i in range(2)]

                q_last_user = {t: None for t in range(NDC)}
                exp_bybuf = [None, None]
                pv_byphat = [None, None, None]
                ou_free = [[], []]
                scores_war = []
                ymt = [None] * NDC
                yst = [None] * NDC
                sum_cp = []
                for qb in range(NJB):
                    q_ld = []
                    for t in range(NDC):
                        q_ld.append(p.dma("sync", qsl[t][:], qs[qb, t],
                                          deps=[q_store[t], q_last_user[t]],
                                          slot="qsl"))
                    nkc = 4 * qb + 4
                    seq = [(t, kc) for t in range(NDC) for kc in range(nkc)]
                    pend = []
                    for i in range(len(seq) + 2):
                        if i < len(seq):
                            t, kc = seq[i]
                            hA, hB = EVENS[t], ODDS[t]
                            m = hA // 8
                            off = max(0, 128 * (kc - 4 * qb))
                            buf = i % 2
                            pbuf = i % 3
                            mA = p.mm(pS[buf][:, off:512],
                                      kb[m][0:64, kc * 128:(kc + 1) * 128],
                                      qsl[t][0:64, off:512], True, True,
                                      deps=[q_ld[-1], k_rope[2 * m],
                                            k_rope[2 * m + 1],
                                            exp_bybuf[buf]] +
                                           (scores_war if t == 0 and kc <= 1
                                            else []))
                            mB = p.mm(pS[buf][:, 512 + off:1024],
                                      kb[m][64:128, kc * 128:(kc + 1) * 128],
                                      qsl[t][64:128, off:512], True, True,
                                      deps=[], tp=(64, 0))
                            q_last_user[t] = mB
                            edeps = [mB, bias_ev, pv_byphat[pbuf]]
                            if off == 0:
                                eB = p.act(Phat[pbuf][:], pS[buf][:], AF.Exp,
                                           deps=edeps,
                                           bias=bias_all[:, t:t + 1])
                            else:
                                p.act(Phat[pbuf][:, off:512],
                                      pS[buf][:, off:512], AF.Exp, deps=edeps,
                                      bias=bias_all[:, t:t + 1])
                                eB = p.act(Phat[pbuf][:, 512 + off:1024],
                                           pS[buf][:, 512 + off:1024],
                                           AF.Exp, deps=[],
                                           bias=bias_all[:, t:t + 1])
                            exp_bybuf[buf] = eB
                            pv_dep = [eB]
                            if kc >= 4 * qb:
                                p.tt(Phat[pbuf][:, off:off + 128],
                                     Phat[pbuf][:, off:off + 128], dmk[:],
                                     ALU.mult, deps=[eB, ld_mask])
                                mkB = p.tt(
                                    Phat[pbuf][:, 512 + off:512 + off + 128],
                                    Phat[pbuf][:, 512 + off:512 + off + 128],
                                    dmk[:], ALU.mult, deps=[])
                                pv_dep = [mkB]
                            pend.append((pv_dep, pbuf, off, t, kc))
                            drive(1)
                        j = i - 2
                        if j >= 0:
                            pv_dep, pbuf, off, t, kc = pend[j]
                            os_ = t % 2
                            gA, gB = EVENS[t] // 4, ODDS[t] // 4
                            pvA = p.mm(ouA[os_][:, off:512],
                                       vb[gA][:, 65 * kc:65 * kc + 65],
                                       Phat[pbuf][:, off:512],
                                       kc == 0, kc == nkc - 1,
                                       deps=pv_dep + [v_ev[kc]] +
                                            (ou_free[os_] if kc == 0 else []))
                            pvB = p.mm(ouB[os_][:, off:512],
                                       vb[gB][:, 65 * kc:65 * kc + 65],
                                       Phat[pbuf][:, 512 + off:1024],
                                       kc == 0, kc == nkc - 1, deps=[])
                            pv_byphat[pbuf] = pvB
                            if kc == nkc - 1:
                                cpA = p.vcopy(oubT[t][0:64, :],
                                              ouA[os_][0:64, :],
                                              deps=[pvB, ymt[t]])
                                cpB = p.vcopy(oubT[t][64:128, :],
                                              ouB[os_][0:64, :], deps=[])
                                sA = p.vcopy(
                                    srows[64:65,
                                          (2 * t) * 512:(2 * t + 1) * 512],
                                    ouA[os_][64:65, :], deps=sum_cp[-1:])
                                sB = p.vcopy(
                                    srows[64:65,
                                          (2 * t + 1) * 512:(2 * t + 2) * 512],
                                    ouB[os_][64:65, :], deps=[])
                                sum_cp.append(sB)
                                ou_free[os_] = [cpA, cpB, sA, sB]
                    lnb = p.act(srows[64:65, :], srows[64:65, :], AF.Ln,
                                deps=[sum_cp[-1]])
                    exb = p.act(srows[64:65, :], srows[64:65, :], AF.Exp,
                                deps=[lnb], scale=-1.0)
                    ym_prev = []
                    scores_war = []
                    for t in range(NDC):
                        ct = p.vcopy(recb[64:65, :],
                                     srows[64:65, 2 * t * 512:(2 * t + 2) * 512],
                                     deps=[exb] + ym_prev)
                        bcA = p.mm(pS[0][:, 0:512], onesr64[64:65, :],
                                   recb[64:65, 0:512], True, True,
                                   deps=[ct, c_ones64, exp_bybuf[0]])
                        bcB = p.mm(pS[1][:, 0:512], onesr64[64:65, :],
                                   recb[64:65, 512:1024], True, True,
                                   deps=[ct, exp_bybuf[1]])
                        ymA = p.tt(ybuf[t % 2][0:64, :], oubT[t][0:64, :],
                                   pS[0][0:64, 0:512], ALU.mult,
                                   deps=[bcA, yst[t]])
                        ymB = p.tt(ybuf[t % 2][64:128, :], oubT[t][64:128, :],
                                   pS[1][0:64, 0:512], ALU.mult,
                                   deps=[bcB])
                        ym_prev = [ymA, ymB]
                        ymt[t] = ymB
                        scores_war += [ymA, ymB]
                        yst[t] = p.dma(
                            "sync", ys[t, :, qb * 512:(qb + 1) * 512],
                            ybuf[t % 2][:], deps=[ymA, ymB],
                            slot=f"ys{t % 2}")
                    scores_war = scores_war[-2:]
                y_done = [y for y in yst if y is not None][-2:]
            p.barrier()

        attn_ev = []
        for hf in range(2):
            c0 = hf * 1024
            with contextlib.ExitStack() as ps:
                yh = [SB(ps, f"yh{hf}_{dc}", [128, 1024], BF16) for dc in range(NDC)]
                wob = [SB(ps, f"wob{hf}_{i}", [128, S], BF16) for i in range(2)]
                abf = [SB(ps, f"abf{hf}_{dc}", [128, 1024], BF16) for dc in range(NDC)]
                sqb = [SB(ps, f"sqb{hf}_{i}", [128, 1024], BF16) for i in range(2)]
                xbuf = [SB(ps, f"xbuf{hf}_{i}", [128, 1024], F32) for i in range(2)]
                tbuf = [SB(ps, f"tbuf{hf}_{i}", [128, 1024], F32) for i in range(2)]
                x2b = [SB(ps, f"x2b{hf}_{i}", [128, 1024], F32) for i in range(2)]
                pwo = [PS(ps, f"pwo{hf}_{i}", [128, 1024]) for i in range(2)]
                pz6 = [PS(ps, f"pz6{hf}_{j}", [1, 512]) for j in range(2)]
                pq6 = [PS(ps, f"pq6{hf}_{j}", [1, 512]) for j in range(2)]
                rows6 = SB(ps, f"rows6{hf}", [1, 5 * 1024], F32)

                y_ld = [p.dma("sync", yh[dc][:], ys[dc, :, c0:c0 + 1024],
                              deps=y_done, slot="yh") for dc in range(NDC)]
                if debug:
                    for dc in range(NDC):
                        p.dma("sync", dbg["d_yh"][dc * 128:(dc + 1) * 128,
                                                  c0:c0 + 1024],
                              yh[dc][:], deps=[y_ld[dc]], slot="dbg")
                evs, sqs, stat6 = [], [], []
                for mc in range(NDC):
                    ld = p.dma("gpsimd",
                               wob[mc % 2][:].rearrange("p (a c) -> p a c", c=128),
                               wo4[mc],
                               deps=[evs[mc - 2]] if mc >= 2 else [],
                               slot=f"wo{mc % 2}")
                    last = None
                    for dc in range(NDC):
                        for j in range(2):
                            last = p.mm(pwo[mc % 2][:, j * 512:(j + 1) * 512],
                                        wob[mc % 2][:, dc * 128:(dc + 1) * 128],
                                        yh[dc][:, j * 512:(j + 1) * 512],
                                        dc == 0, dc == NDC - 1,
                                        deps=[ld, y_ld[-1]] +
                                             ([sqs[mc - 2]] if mc >= 2 and dc == 0
                                              else []))
                    ev = p.scopy(abf[mc][:], pwo[mc % 2][:], deps=[last])
                    sq = p.act(sqb[mc % 2][:], pwo[mc % 2][:], AF.Square,
                               deps=[stat6[mc - 2]] if mc >= 2 else [])
                    evs.append(ev)
                    sqs.append(sq)
                    slast = None
                    for j in range(2):
                        p.mm(pz6[j][:], ones_bf[:],
                             abf[mc][:, j * 512:(j + 1) * 512],
                             mc == 0, mc == NDC - 1, deps=[ev])
                        slast = p.mm(pq6[j][:], ones_bf[:],
                                     sqb[mc % 2][:, j * 512:(j + 1) * 512],
                                     mc == 0, mc == NDC - 1, deps=[sq])
                    stat6.append(slast)
                    drive(4)
                nr6 = ln_rows(rows6, [(pz6[j][:], j * 512) for j in range(2)],
                              [(pq6[j][:], j * 512) for j in range(2)],
                              1024, deps=[stat6[-1]])
                bc = bcast_rows(rows6, pwo[0], pwo[1], 1024,
                                deps=[nr6, evs[-1], sqs[-1]])
                x2_st = [None, None]
                for mc in range(NDC):
                    xld = p.dma("sync", xbuf[mc % 2][:],
                                xt[mc * 128:(mc + 1) * 128, c0:c0 + 1024],
                                deps=[x2_st[mc % 2]], slot=f"xb{mc % 2}")
                    t1 = p.tt(tbuf[mc % 2][:], abf[mc][:], pwo[0][:], ALU.mult,
                              deps=(bc if mc == 0 else []))
                    t2 = p.tt(tbuf[mc % 2][:], tbuf[mc % 2][:], pwo[1][:],
                              ALU.add, deps=[t1])
                    t3 = p.tt(x2b[mc % 2][:], tbuf[mc % 2][:], xbuf[mc % 2][:],
                              ALU.add, deps=[xld, x2_st[mc % 2], t2])
                    x2_st[mc % 2] = p.dma(
                        "sync", x2s[mc * 128:(mc + 1) * 128, c0:c0 + 1024],
                        x2b[mc % 2][:], deps=[t3], slot=f"x2{mc % 2}")
                attn_ev.extend(s for s in x2_st if s)
                if debug:
                    for mc in range(NDC):
                        p.dma("sync",
                              dbg["d_attn2"][mc * 128:(mc + 1) * 128,
                                             c0:c0 + 1024],
                              abf[mc][:], deps=[evs[mc]])
            p.barrier()

        drain_wstream()
        qscope.close()
        assert quant_done[0] is not None and quant_done[1] is not None
        half_guard = []
        p9o = contextlib.ExitStack()
        xbuf9 = [SB(p9o, f"xbuf9_{i}", [128, 1024], F32) for i in range(2)]
        tbuf9 = [SB(p9o, f"tbuf9_{i}", [128, 1024], F32) for i in range(2)]
        ob9 = [SB(p9o, f"ob9_{i}", [128, 1024], F32) for i in range(2)]
        o_st = [None, None]
        for hf in range(2):
            c0 = hf * 1024
            with contextlib.ExitStack() as ph:
                hid = SB(ph, f"hid{hf}", [128, NFH * 1024], BF16)
                rowsh = SB(ph, f"rowsh{hf}", [1, 5 * 1024], F32)
                hid_ev = []
                with contextlib.ExitStack() as psx:
                    xb2 = [SB(psx, f"xb2{hf}_{dc}", [128, 1024], BF16)
                           for dc in range(NDC)]
                    x2_ld = [p.dma("gpsimd", xb2[dc][:],
                                   x2s[dc * 128:(dc + 1) * 128, c0:c0 + 1024],
                                   deps=attn_ev + half_guard, slot="x2l")
                             for dc in range(NDC)]
                    with contextlib.ExitStack() as ps:
                        sq9 = [SB(ps, f"sq9{hf}_{i}", [128, 1024], BF16)
                               for i in range(2)]
                        tmp9 = [SB(ps, f"tmp9{hf}_{i}", [128, 1024], F32)
                                for i in range(2)]
                        pz9 = [PS(ps, f"pz9{hf}_{j}", [1, 512]) for j in range(2)]
                        pq9 = [PS(ps, f"pq9{hf}_{j}", [1, 512]) for j in range(2)]
                        pb9 = [PS(ps, f"pb9{hf}_{i}", [128, 1024]) for i in range(2)]
                        stat9 = []
                        for dc in range(NDC):
                            sq = p.act(sq9[dc % 2][:], xb2[dc][:], AF.Square,
                                       deps=[x2_ld[-1]] +
                                            ([stat9[dc - 2]] if dc >= 2 else []))
                            last = None
                            for j in range(2):
                                p.mm(pz9[j][:], ones_bf[:],
                                     xb2[dc][:, j * 512:(j + 1) * 512],
                                     dc == 0, dc == NDC - 1, deps=[sq])
                                last = p.mm(pq9[j][:], ones_bf[:],
                                            sq9[dc % 2][:, j * 512:(j + 1) * 512],
                                            dc == 0, dc == NDC - 1, deps=[sq])
                            stat9.append(last)
                        nr9 = ln_rows(rowsh, [(pz9[j][:], j * 512)
                                       for j in range(2)],
                                      [(pq9[j][:], j * 512)
                                       for j in range(2)],
                                      1024, deps=stat9)
                        bc9 = bcast_rows(rowsh, pb9[0], pb9[1], 1024, deps=[nr9])
                        xn2_ev = []
                        for dc in range(NDC):
                            t1 = p.tt(tmp9[dc % 2][:], xb2[dc][:], pb9[0][:],
                                      ALU.mult, deps=(bc9 if dc == 0 else []))
                            xn2_ev.append(p.tt(xb2[dc][:], tmp9[dc % 2][:],
                                               pb9[1][:], ALU.add, deps=[t1]))
                        if debug:
                            for dc in range(NDC):
                                p.dma("sync",
                                      dbg["d_xn2"][dc * 128:(dc + 1) * 128,
                                                   c0:c0 + 1024],
                                      xb2[dc][:], deps=[xn2_ev[dc]])
                    p.barrier()

                    with contextlib.ExitStack() as pw:
                        qw1b = [SB(pw, f"qw1b{hf}_{i}", [128, S], BF16)
                                for i in range(4)]
                        qw2b = [SB(pw, f"qw2b{hf}_{i}", [128, S], BF16)
                                for i in range(4)]
                        sigb = [SB(pw, f"sigb{hf}_{i}", [128, 1024], BF16)
                                for i in range(2)]
                        px1 = [PS(pw, f"px1{hf}_{i}", [128, 1024]) for i in range(2)]
                        px2 = [PS(pw, f"px2{hf}_{i}", [128, 1024]) for i in range(2)]
                        x2_mm = []
                        for hc in range(NFH):
                            ld1 = p.dma("sync",
                                        qw1b[hc % 4][:].rearrange(
                                            "p (a c) -> p a c", c=128),
                                        qw1s[hc],
                                        deps=quant_done +
                                             ([x2_mm[hc - 4]] if hc >= 4 else []),
                                        slot=f"w1{hc % 4}")
                            ld2 = p.dma("sync",
                                        qw2b[hc % 4][:].rearrange(
                                            "p (a c) -> p a c", c=128),
                                        qw2s[hc],
                                        deps=[x2_mm[hc - 4]] if hc >= 4 else [],
                                        slot=f"w2{hc % 4}")
                            m1 = m2 = None
                            for dc in range(NDC):
                                for j in range(2):
                                    m1 = p.mm(px1[hc % 2][:, j * 512:(j + 1) * 512],
                                              qw1b[hc % 4][:,
                                                           dc * 128:(dc + 1) * 128],
                                              xb2[dc][:, j * 512:(j + 1) * 512],
                                              dc == 0, dc == NDC - 1,
                                              deps=[ld1, xn2_ev[dc]] +
                                                   ([hid_ev[hc - 2]]
                                                    if hc >= 2 and dc == 0
                                                    else []))
                            for dc in range(NDC):
                                for j in range(2):
                                    m2 = p.mm(px2[hc % 2][:, j * 512:(j + 1) * 512],
                                              qw2b[hc % 4][:,
                                                           dc * 128:(dc + 1) * 128],
                                              xb2[dc][:, j * 512:(j + 1) * 512],
                                              dc == 0, dc == NDC - 1,
                                              deps=[ld2, xn2_ev[dc]])
                            x2_mm.append(m2)
                            sg = p.act(sigb[hc % 2][:], px1[hc % 2][:], AF.Sigmoid,
                                       deps=[m1, m2] +
                                            ([hid_ev[hc - 2]] if hc >= 2 else []))
                            hid_ev.append(
                                p.tt(hid[:, hc * 1024:(hc + 1) * 1024],
                                     sigb[hc % 2][:], px2[hc % 2][:], ALU.mult,
                                     deps=[sg]))
                        if debug and hf == 0:
                            for hc in range(NFH):
                                p.dma("sync",
                                      dbg["d_hid0"][:, hc * 1024:(hc + 1) * 1024],
                                      hid[:, hc * 1024:(hc + 1) * 1024],
                                      deps=[hid_ev[hc]])
                with contextlib.ExitStack() as pw:
                    qw3b = [SB(pw, f"qw3b{hf}_{i}", [128, FF], BF16) for i in range(2)]
                    fbf = [SB(pw, f"fbf{hf}_{dc}", [128, 1024], BF16)
                           for dc in range(NDC)]
                    sqf = [SB(pw, f"sqf{hf}_{i}", [128, 1024], BF16) for i in range(2)]
                    pw3 = [PS(pw, f"pw3{hf}_{i}", [128, 1024]) for i in range(2)]
                    pzf = [PS(pw, f"pzf{hf}_{j}", [1, 512]) for j in range(2)]
                    pqf = [PS(pw, f"pqf{hf}_{j}", [1, 512]) for j in range(2)]

                    f_ev, f_sq, statf = [], [], []
                    t3_last = None
                    for md in range(NDC):
                        ld = p.dma("sync",
                                   qw3b[md % 2][:].rearrange("p (a c) -> p a c",
                                                             c=128),
                                   qw3s[md],
                                   deps=quant_done +
                                        ([x2_mm[-1], hid_ev[-1]]
                                         if md < 2 else [f_ev[md - 2]]),
                                   slot=f"w3{md % 2}")
                        last = None
                        for hc in range(NFH):
                            for j in range(2):
                                last = p.mm(pw3[md % 2][:, j * 512:(j + 1) * 512],
                                            qw3b[md % 2][:,
                                                         hc * 128:(hc + 1) * 128],
                                            hid[:, hc * 1024 + j * 512:
                                                hc * 1024 + (j + 1) * 512],
                                            hc == 0, hc == NFH - 1,
                                            deps=[ld, hid_ev[hc]] +
                                                 ([f_sq[md - 2]]
                                                  if md >= 2 and hc == 0 else []))
                        ev = p.scopy(fbf[md][:], pw3[md % 2][:], deps=[last])
                        sq = p.act(sqf[md % 2][:], pw3[md % 2][:], AF.Square,
                                   deps=[statf[md - 2]] if md >= 2
                                        else [hid_ev[-1]])
                        f_ev.append(ev)
                        f_sq.append(sq)
                        if debug:
                            p.dma("sync",
                                  dbg["d_ffn"][md * 128:(md + 1) * 128,
                                               c0:c0 + 1024],
                                  fbf[md][:], deps=[ev], slot="dbg")
                        slast = None
                        for j in range(2):
                            p.mm(pzf[j][:], ones_bf[:],
                                 fbf[md][:, j * 512:(j + 1) * 512],
                                 md == 0, md == NDC - 1, deps=[ev])
                            slast = p.mm(pqf[j][:], ones_bf[:],
                                         sqf[md % 2][:, j * 512:(j + 1) * 512],
                                         md == 0, md == NDC - 1, deps=[sq])
                        statf.append(slast)
                    nrf = ln_rows(rowsh, [(pzf[j][:], j * 512) for j in range(2)],
                                  [(pqf[j][:], j * 512) for j in range(2)],
                                  1024, deps=[statf[-1]])
                    bcf = bcast_rows(rowsh, pw3[0], pw3[1], 1024,
                                     deps=[nrf, f_ev[-1], f_sq[-1]])
                    for md in range(NDC):
                        xld = p.dma("sync", xbuf9[md % 2][:],
                                    x2s[md * 128:(md + 1) * 128, c0:c0 + 1024],
                                    deps=attn_ev + [o_st[md % 2], hid_ev[-1]],
                                    slot=f"xb{md % 2}")
                        t1 = p.tt(tbuf9[md % 2][:], fbf[md][:], pw3[0][:],
                                  ALU.mult, deps=(bcf if md == 0 else []))
                        t2 = p.tt(tbuf9[md % 2][:], tbuf9[md % 2][:], pw3[1][:],
                                  ALU.add, deps=[t1])
                        t3 = p.tt(ob9[md % 2][:], tbuf9[md % 2][:],
                                  xbuf9[md % 2][:], ALU.add,
                                  deps=[xld, o_st[md % 2], t2])
                        t3_last = t3
                        o_st[md % 2] = p.dma(
                            "sync", outT[md * 128:(md + 1) * 128, c0:c0 + 1024],
                            ob9[md % 2][:], deps=[t3], slot=f"o{md % 2}")
                half_guard = [x2_mm[-1], hid_ev[-1], f_ev[-1], f_sq[-1],
                              nrf, statf[-1], t3_last]

        p9o.close()
        p.barrier()
        for s in sorted(set(_ENG_SEM.values())) + p.dma_sems():
            sems[s] = E(nc.semaphore(s))
        p.finish(block, sems)
    return nc, dbg


def _host_inputs(x, Wq, Wk, Wv, Wo, w1, w2, w3):
    qcols = []
    for t in range(16):
        for h in (EVENS[t], ODDS[t]):
            qcols += [64 * h + 2 * j for j in range(32)]
            qcols += [64 * h + 2 * j + 1 for j in range(32)]
    kcols = []
    for g in range(NKV):
        kcols += [64 * g + 2 * j for j in range(32)]
        kcols += [64 * g + 2 * j + 1 for j in range(32)]
    worows = []
    for t in range(16):
        for h in (EVENS[t], ODDS[t]):
            worows += [64 * h + j for j in range(64)]

    wq_p = np.ascontiguousarray(Wq[:, qcols])
    wk_p = np.ascontiguousarray(Wk[:, kcols])
    wo_p = np.ascontiguousarray(Wo[worows, :])
    wq4 = np.ascontiguousarray(
        wq_p.reshape(NDC, 128, NDC, 128).transpose(2, 1, 0, 3))
    wk4 = np.ascontiguousarray(
        wk_p.reshape(NDC, 128, 4, 128).transpose(2, 1, 0, 3))
    wo4 = np.ascontiguousarray(
        wo_p.reshape(NDC, 128, NDC, 128).transpose(2, 1, 0, 3))
    w14 = np.ascontiguousarray(
        w1.reshape(NFH, 128, NDC, 128).transpose(0, 3, 2, 1))
    w24 = np.ascontiguousarray(
        w2.reshape(NFH, 128, NDC, 128).transpose(0, 3, 2, 1))
    w34 = np.ascontiguousarray(
        w3.reshape(NDC, 128, NFH, 128).transpose(0, 3, 2, 1))

    half = 32
    inv = 1.0 / (10000.0 ** (np.arange(half, dtype=np.float64) / half))
    ang = np.outer(inv, np.arange(S, dtype=np.float64))
    s8 = 1.0 / np.sqrt(np.float64(8.0))
    cosf = np.cos(ang) * s8
    sinf = np.sin(ang) * s8
    tabA = np.ascontiguousarray(np.tile(cosf, (4, 1)).astype(np.float32))
    tabB = np.ascontiguousarray(
        np.tile(np.concatenate([-sinf, sinf], axis=0), (2, 1)).astype(np.float32))

    ii, jj = np.meshgrid(np.arange(128), np.arange(128), indexing="ij")
    dmsk = np.ascontiguousarray((jj >= ii).astype(ml_dtypes.bfloat16))

    shared = {"wq4": wq4, "wk4": wk4, "wv2": np.ascontiguousarray(Wv),
              "wo4": wo4, "w14": w14, "w24": w24, "w34": w34,
              "tabA": tabA, "tabB": tabB, "dmask": dmsk}
    return [dict(shared, xt=np.ascontiguousarray(x[b].T)) for b in range(8)]


def kernel(L0_attn_x, L0_attn_Wq, L0_attn_Wk, L0_attn_Wv, L0_attn_Wo,
           L0_ffn_w1, L0_ffn_w2, L0_ffn_w3):
    x = np.asarray(L0_attn_x, dtype=np.float32)
    in_maps = _host_inputs(x, np.asarray(L0_attn_Wq, dtype=np.float32),
                           np.asarray(L0_attn_Wk, dtype=np.float32),
                           np.asarray(L0_attn_Wv, dtype=np.float32),
                           np.asarray(L0_attn_Wo, dtype=np.float32),
                           np.asarray(L0_ffn_w1, dtype=np.float32),
                           np.asarray(L0_ffn_w2, dtype=np.float32),
                           np.asarray(L0_ffn_w3, dtype=np.float32))
    nc, _ = build(debug=False)
    res = run_bass_kernel_spmd(nc, in_maps, core_ids=list(range(8)))
    out = np.stack([np.asarray(r["outT"], dtype=np.float32).T
                    for r in res.results], axis=0)
    return np.ascontiguousarray(out)
